# version 03
# baseline (speedup 1.0000x reference)
from contextlib import ExitStack

import numpy as np
import concourse.bass as bass
import concourse.mybir as mybir
from concourse.bass_utils import run_bass_kernel_spmd

F32 = mybir.dt.float32
BF16 = mybir.dt.bfloat16
I32 = mybir.dt.int32
AF = mybir.ActivationFunctionType
ALU = mybir.AluOpType
AX = mybir.AxisListType

NCORES = 8
TT = 2048
NTT = TT // 128
D = 2048
KC = D // 128
EPS = 1e-5
NEG = -30000.0
QSCALE = 1.0 / np.sqrt(128.0)

D_INNER = 4096
D_XB = 1024
NHEADS = 64
NKV = 16
HDIM = 64
DSTATE = 64
INTER0 = 8192
DINPROJ = 10304
NEXP = 8
AHEADS = 16
AHD = 128
GINTER = 5504
GTILES = 43
SEQ = 4096
BATCH = 4


class Sem:

    def __init__(self, h):
        self.h = h
        self.n = 0

    def inc(self, instr, k=1):
        self.n += k
        return instr.then_inc(self.h, k)


def fence(eng, sf: Sem, instr):
    sf.inc(instr)
    eng.wait_ge(sf.h, sf.n)


def reg_consts(nc, v, sb, values):
    for i, val in enumerate(values):
        v.memset(sb[:, i:i + 1], float(val))
        nc.const_aps.register(F32, float(val), sb[:, i:i + 1])


def ln_v(nc, v, sv: Sem, ss: Sem, sf: Sem, x_ap, out_ap, n, st,
         wrep=None, brep=None, out2=None):
    nch = (n + 511) // 512
    mu, e2, var, m2 = st[:, 0:1], st[:, 1:2], st[:, 2:3], st[:, 3:4]
    r = st[:, 12:13]
    sv.inc(v.reduce_sum(mu, x_ap, axis=AX.X))
    ss.n += 1
    v.wait_ge(ss.h, ss.n)
    fence(v, sf, v.reduce_sum(e2, st[:, 4:4 + nch], axis=AX.X))
    v.tensor_scalar_mul(mu, mu, 1.0 / n)
    fence(v, sf, v.tensor_scalar_mul(e2, e2, 1.0 / n))
    fence(v, sf, v.tensor_mul(m2, mu, mu))
    sv.inc(v.tensor_sub(var, e2, m2))
    ss.n += 1
    v.wait_ge(ss.h, ss.n)
    ins = v.tensor_scalar(out_ap, x_ap, mu, r, ALU.subtract, ALU.mult)
    if wrep is not None:
        fence(v, sf, ins)
        ins = v.tensor_mul(out_ap, out_ap, wrep)
    if brep is not None:
        fence(v, sf, ins)
        ins = v.tensor_add(out_ap, out_ap, brep)
    if out2 is not None:
        fence(v, sf, ins)
        ins = v.tensor_copy(out2, out_ap)
    return ins


def ln_sc(nc, sc, sv: Sem, ss: Sem, sf2: Sem, x_ap, n, sq, st):
    nch = (n + 511) // 512
    var, r = st[:, 2:3], st[:, 12:13]
    sv.n += 1
    sc.wait_ge(sv.h, sv.n)
    ins = None
    for j in range(nch):
        w = min(512, n - j * 512)
        ins = sc.activation(sq[:, j * 512:j * 512 + w],
                            x_ap[:, j * 512:j * 512 + w], AF.Square,
                            accum_out=st[:, 4 + j:5 + j])
    ss.inc(ins)
    sv.n += 1
    sc.wait_ge(sv.h, sv.n)
    fence(sc, sf2, sc.activation(r, var, AF.Ln, bias=EPS))
    ss.inc(sc.activation(r, r, AF.Exp, scale=-0.5))


def build_add_ln(two, gamma, out_bf, double_b=False):
    nc = bass.Bass("TRN2", target_bir_lowering=False)
    a = nc.dram_tensor("a", [TT, D], F32, kind="ExternalInput")
    b = nc.dram_tensor("b", [TT, D], F32, kind="ExternalInput") if two else None
    lnw = nc.dram_tensor("lnw", [128, D], F32, kind="ExternalInput") if gamma else None
    lnb = nc.dram_tensor("lnb", [128, D], F32, kind="ExternalInput") if gamma else None
    o = nc.dram_tensor("o", [TT, D], F32, kind="ExternalOutput")
    ob = nc.dram_tensor("ob", [TT, D], BF16, kind="ExternalOutput") if out_bf else None

    es1 = ExitStack()
    with es1:
        s_ld = es1.enter_context(nc.semaphore("ld"))
        s_sv = es1.enter_context(nc.semaphore("sv"))
        s_ss = es1.enter_context(nc.semaphore("ss"))
        s_ev = es1.enter_context(nc.semaphore("ev"))
        s_st = es1.enter_context(nc.semaphore("st"))
        s_sf = es1.enter_context(nc.semaphore("sf"))
        s_sf2 = es1.enter_context(nc.semaphore("sf2"))
        s_lp = es1.enter_context(nc.semaphore("lp"))
        s_t0 = es1.enter_context(nc.semaphore("t0"))
        s_t1 = es1.enter_context(nc.semaphore("t1"))
        s_s0 = es1.enter_context(nc.semaphore("s0"))
        s_s1 = es1.enter_context(nc.semaphore("s1"))
        asb = es1.enter_context(nc.sbuf_tensor("as_", [128, 2 * D], F32))
        bsb = es1.enter_context(nc.sbuf_tensor("bs", [128, 2 * D], F32))
        lnws = es1.enter_context(nc.sbuf_tensor("lnws", [128, D], F32))
        lnbs = es1.enter_context(nc.sbuf_tensor("lnbs", [128, D], F32))
        osb = es1.enter_context(nc.sbuf_tensor("os", [128, 2 * D], F32))
        obsb = es1.enter_context(nc.sbuf_tensor("obs", [128, 2 * D], BF16))
        sq = es1.enter_context(nc.sbuf_tensor("sq", [128, D], BF16))
        stt = es1.enter_context(nc.sbuf_tensor("stt", [128, 16], F32))
        cst = es1.enter_context(nc.sbuf_tensor("cst", [128, 2], F32))
        ld, sv, ss, ev, st = (Sem(x) for x in (s_ld, s_sv, s_ss, s_ev, s_st))
        sf, sf2 = Sem(s_sf), Sem(s_sf2)
        per = 32 if two else 16
        pst = 32 if out_bf else 16
        ts_ = [s_t0, s_t1]
        ss_ = [s_s0, s_s1]
        with nc.Block() as blk:
            @blk.vector
            def _(v):
                reg_consts(nc, v, cst, [EPS, -0.5])

            @blk.sync
            def _(s):
                if gamma:
                    s.dma_start(out=lnws[:, :], in_=lnw[:, :]).then_inc(s_lp, 16)
                    s.dma_start(out=lnbs[:, :], in_=lnb[:, :]).then_inc(s_lp, 16)
                for t in range(NTT):
                    if t >= 2:
                        s.wait_ge(ev.h, t - 1)
                    p = (t % 2) * D
                    s.dma_start(out=asb[:, p:p + D],
                                in_=a[t * 128:(t + 1) * 128, :]
                                ).then_inc(ts_[t % 2], 16)
                    if two:
                        s.dma_start(out=bsb[:, p:p + D],
                                    in_=b[t * 128:(t + 1) * 128, :]
                                    ).then_inc(ts_[t % 2], 16)

            @blk.vector
            def _(v):
                if gamma:
                    v.wait_ge(s_lp, 32)
                for t in range(NTT):
                    v.wait_ge(ts_[t % 2], per * (t // 2 + 1))
                    p = (t % 2) * D
                    xa = asb[:, p:p + D]
                    if t >= 2:
                        v.wait_ge(ss_[t % 2], pst * (t // 2))
                    if two:
                        fence(v, sf, v.tensor_add(xa, xa, bsb[:, p:p + D]))
                        if double_b:
                            fence(v, sf, v.tensor_add(xa, xa, bsb[:, p:p + D]))
                    ins = ln_v(nc, v, sv, ss, sf, xa, osb[:, p:p + D],
                               D, stt, lnws[:, :] if gamma else None,
                               lnbs[:, :] if gamma else None,
                               obsb[:, p:p + D] if out_bf else None)
                    ev.inc(ins)

            @blk.scalar
            def _(sc):
                sv2, ss2 = Sem(s_sv), Sem(s_ss)
                for t in range(NTT):
                    sc.wait_ge(ts_[t % 2], per * (t // 2 + 1))
                    p = (t % 2) * D
                    ln_sc(nc, sc, sv2, ss2, sf2, asb[:, p:p + D], D,
                          sq, stt)

            @blk.gpsimd
            def _(g):
                for t in range(NTT):
                    g.wait_ge(ev.h, t + 1)
                    p = (t % 2) * D
                    g.dma_start(out=o[t * 128:(t + 1) * 128, :],
                                in_=osb[:, p:p + D]).then_inc(ss_[t % 2], 16)
                    if out_bf:
                        g.dma_start(out=ob[t * 128:(t + 1) * 128, :],
                                    in_=obsb[:, p:p + D]
                                    ).then_inc(ss_[t % 2], 16)
                g.wait_ge(ss_[0], pst * 8)
                g.wait_ge(ss_[1], pst * 8)
    return nc


def build_proj(name, n_out, wcast, extra=None):
    nc = bass.Bass("TRN2", target_bir_lowering=False)
    xT = nc.dram_tensor("xT", [D, TT], BF16, kind="ExternalInput")
    wT = nc.dram_tensor("wT", [D, n_out], F32 if wcast else BF16,
                        kind="ExternalInput")
    bias = None
    if extra == "qkv":
        bias = nc.dram_tensor("bias", [128, n_out], F32, kind="ExternalInput")
        NB = 16
    else:
        NB = 0
    if extra == "inproj":
        dtb = nc.dram_tensor("dtb", [128, 64], F32, kind="ExternalInput")
        mm = nc.dram_tensor("mm", [TT, 10240], BF16, kind="ExternalOutput")
        dto = nc.dram_tensor("dt", [TT, 64], F32, kind="ExternalOutput")
    else:
        odt = BF16 if extra in ("bf16", "qkv") else F32
        mm = nc.dram_tensor("mm", [TT, n_out], odt, kind="ExternalOutput")

    ccs = []
    c0 = 0
    while c0 < n_out:
        w = min(512, n_out - c0)
        ccs.append((c0, w))
        c0 += w

    es2 = ExitStack()
    with es2:
        s_ld = es2.enter_context(nc.semaphore("ld"))
        s_wl = es2.enter_context(nc.semaphore("wl"))
        s_mm = es2.enter_context(nc.semaphore("mmv"))
        s_ev = es2.enter_context(nc.semaphore("ev"))
        s_st = es2.enter_context(nc.semaphore("st"))
        s_pp = es2.enter_context(nc.semaphore("pp"))
        s_sf = es2.enter_context(nc.semaphore("sf"))
        s_sf2 = es2.enter_context(nc.semaphore("sfv"))
        s_lx = es2.enter_context(nc.semaphore("lx"))
        s_ldt = es2.enter_context(nc.semaphore("ldt"))
        s_w0 = es2.enter_context(nc.semaphore("w0"))
        s_w1 = es2.enter_context(nc.semaphore("w1"))
        s_o0 = es2.enter_context(nc.semaphore("o0"))
        s_o1 = es2.enter_context(nc.semaphore("o1"))
        s_o2 = es2.enter_context(nc.semaphore("o2"))
        s_o3 = es2.enter_context(nc.semaphore("o3"))
        xTs = es2.enter_context(nc.sbuf_tensor("xTs", [128, KC * TT], BF16))
        wbuf = es2.enter_context(nc.sbuf_tensor("wbuf", [128, 2 * KC * 512], BF16))
        osb = es2.enter_context(nc.sbuf_tensor("ob", [128, 4 * 512], F32))
        obb = es2.enter_context(nc.sbuf_tensor("obb", [128, 4 * 512], BF16))
        biasb = es2.enter_context(nc.sbuf_tensor("biasb", [128, 2 * 512], F32))
        dtbs = es2.enter_context(nc.sbuf_tensor("dtbs", [128, 64], F32))
        sp1 = es2.enter_context(nc.sbuf_tensor("sp1", [128, 128], F32))
        sp2 = es2.enter_context(nc.sbuf_tensor("sp2", [128, 128], F32))
        ps0 = es2.enter_context(nc.psum_tensor("ps0", [128, 512], F32))
        ps1 = es2.enter_context(nc.psum_tensor("ps1", [128, 512], F32))
        ps2 = es2.enter_context(nc.psum_tensor("ps2", [128, 512], F32))
        ps3 = es2.enter_context(nc.psum_tensor("ps3", [128, 512], F32))
        ps = [ps0, ps1, ps2, ps3]
        ld, wl, mmv, ev, st, pp = (Sem(x) for x in
                                   (s_ld, s_wl, s_mm, s_ev, s_st, s_pp))

        ws_ = [s_w0, s_w1]
        os_ = [s_o0, s_o1, s_o2, s_o3]
        with nc.Block() as blk:
            @blk.sync
            def _(s):
                s.dma_start(out=xTs.ap().rearrange("p (k t) -> p k t", t=TT),
                            in_=xT.ap().rearrange("(k p) t -> p k t", p=128)
                            ).then_inc(s_lx, 16)
                if extra == "inproj":
                    s.dma_start(out=dtbs[:, :], in_=dtb[:, :]
                                ).then_inc(s_ldt, 16)

        def wload(g, i):
            c0, w = ccs[i]
            buf = wbuf[:, (i % 2) * KC * 512:(i % 2) * KC * 512 + KC * w]
            g.dma_start(out=buf.rearrange("p (k n) -> p k n", n=w),
                        in_=wT[:, c0:c0 + w].rearrange(
                            "(k p) n -> p k n", p=128)
                        ).then_inc(ws_[i % 2], 16)

        with nc.Block() as blk:
            @blk.gpsimd
            def _(g):
                wload(g, 0)
                if len(ccs) > 1:
                    wload(g, 1)
                for i, (c0, w) in enumerate(ccs):
                    for t in range(NTT):
                        done = i * NTT + t
                        g.wait_ge(ev.h, done + 1)
                        if extra == "inproj" and c0 == 10240:
                            src_ = osb[:, (t % 4) * 512:(t % 4) * 512 + 64]
                            dst = dto[t * 128:(t + 1) * 128, :]
                        elif extra is None:
                            src_ = osb[:, (t % 4) * 512:(t % 4) * 512 + w]
                            dst = mm[t * 128:(t + 1) * 128, c0:c0 + w]
                        else:
                            src_ = obb[:, (t % 4) * 512:(t % 4) * 512 + w]
                            dst = mm[t * 128:(t + 1) * 128, c0:c0 + w]
                        g.dma_start(out=dst, in_=src_
                                    ).then_inc(os_[done % 4], 16)
                    if i + 2 < len(ccs):
                        wload(g, i + 2)
                for q in range(4):
                    tot = len(ccs) * NTT
                    g.wait_ge(os_[q], 16 * ((tot - q + 3) // 4))

            @blk.sync
            def _(s):
                if extra == "qkv":
                    for i, (c0, w) in enumerate(ccs):
                        if i >= 2:
                            s.wait_ge(ev.h, (i - 1) * NTT)
                        s.dma_start(out=biasb[:, (i % 2) * 512:(i % 2) * 512 + w],
                                    in_=bias[:, c0:c0 + w]
                                    ).then_inc(ws_[i % 2], 16)

            @blk.tensor
            def _(te):
                te.wait_ge(s_lx, 16)
                for i, (c0, w) in enumerate(ccs):
                    te.wait_ge(ws_[i % 2], (i // 2 + 1) * (16 + NB))
                    buf = wbuf[:, (i % 2) * KC * 512:]
                    for t in range(NTT):
                        done = i * NTT + t
                        if done >= 4:
                            te.wait_ge(ev.h, done - 3)
                        p = ps[t % 4]
                        for k in range(KC):
                            ins = te.matmul(
                                p[:, 0:w],
                                xTs[:, k * TT + t * 128:k * TT + (t + 1) * 128],
                                buf[:, k * w:(k + 1) * w],
                                start=(k == 0), stop=(k == KC - 1))
                        mmv.inc(ins)

            @blk.vector
            def _(v):
                svf = Sem(s_sf2)
                for i, (c0, w) in enumerate(ccs):
                    for t in range(NTT):
                        done = i * NTT + t
                        v.wait_ge(mmv.h, done + 1)
                        if done >= 4:
                            v.wait_ge(os_[done % 4], 16 * (done // 4))
                        p = ps[t % 4]
                        ob4 = (osb if extra is None else obb)[:, (t % 4) * 512:
                                                              (t % 4) * 512 + w]
                        if extra == "inproj" and c0 == 10240:
                            v.wait_ge(s_ldt, 16)
                            q = (t % 2) * 64
                            pp.inc(v.tensor_add(sp1[:, q:q + 64], p[:, 0:64],
                                                dtbs[:, :]))
                            pp.n += 1
                            v.wait_ge(pp.h, pp.n)
                            fence(v, svf, v.tensor_relu(sp1[:, q:q + 64],
                                                        sp1[:, q:q + 64]))
                            osb_ = osb[:, (t % 4) * 512:(t % 4) * 512 + 64]
                            ev.inc(v.tensor_add(osb_, sp1[:, q:q + 64],
                                                sp2[:, q:q + 64]))
                        elif extra == "qkv":
                            bsl = biasb[:, (i % 2) * 512:(i % 2) * 512 + w]
                            if c0 < 2048:
                                fence(v, svf, v.tensor_add(ob4, p[:, 0:w],
                                                           bsl))
                                ev.inc(v.tensor_scalar_mul(ob4, ob4, QSCALE))
                            else:
                                ev.inc(v.tensor_add(ob4, p[:, 0:w], bsl))
                        else:
                            ev.inc(v.tensor_copy(ob4, p[:, 0:w]))

            @blk.scalar
            def _(sc):
                if extra == "inproj":
                    scf = Sem(s_sf)
                    pp2 = 0
                    for t in range(NTT):
                        q = (t % 2) * 64
                        pp2 += 1
                        sc.wait_ge(s_pp, pp2)
                        fence(sc, scf, sc.activation(
                            sp2[:, q:q + 64], sp1[:, q:q + 64], AF.Abs))
                        fence(sc, scf, sc.activation(
                            sp2[:, q:q + 64], sp2[:, q:q + 64], AF.Exp,
                            scale=-1.0))
                        sc.activation(sp2[:, q:q + 64], sp2[:, q:q + 64],
                                      AF.Ln, bias=1.0).then_inc(s_pp, 1)
                        pp2 += 1


    return nc


def build_gate0():
    nc = bass.Bass("TRN2", target_bir_lowering=False)
    y = nc.dram_tensor("y", [TT, D_INNER], BF16, kind="ExternalInput")
    xh = nc.dram_tensor("xh", [TT, D_INNER], BF16, kind="ExternalInput")
    z = nc.dram_tensor("z", [TT, D_INNER], BF16, kind="ExternalInput")
    dpr = nc.dram_tensor("dpr", [128, D_INNER], F32, kind="ExternalInput")
    nwr = nc.dram_tensor("nwr", [128, D_INNER], F32, kind="ExternalInput")
    yf = nc.dram_tensor("yf", [TT, D_INNER], BF16, kind="ExternalOutput")
    N = D_INNER
    es3 = ExitStack()
    with es3:
        s_ld = es3.enter_context(nc.semaphore("ld"))
        s_dv = es3.enter_context(nc.semaphore("dv"))
        s_ac = es3.enter_context(nc.semaphore("ac"))
        s_st = es3.enter_context(nc.semaphore("st"))
        ys = es3.enter_context(nc.sbuf_tensor("ys", [128, 2 * N], BF16))
        xs = es3.enter_context(nc.sbuf_tensor("xs", [128, 2 * N], BF16))
        zs = es3.enter_context(nc.sbuf_tensor("zs", [128, 2 * N], BF16))
        dps = es3.enter_context(nc.sbuf_tensor("dps", [128, N], F32))
        nws = es3.enter_context(nc.sbuf_tensor("nws", [128, N], F32))
        zb = es3.enter_context(nc.sbuf_tensor("zb", [128, N], BF16))
        t1 = es3.enter_context(nc.sbuf_tensor("t1", [128, N], F32))
        tr = es3.enter_context(nc.sbuf_tensor("tr", [128, N], BF16))
        ob = es3.enter_context(nc.sbuf_tensor("ob", [128, 2 * N], BF16))
        stt = es3.enter_context(nc.sbuf_tensor("stt", [128, 16], F32))
        cst = es3.enter_context(nc.sbuf_tensor("cst", [128, 2], F32))
        s_fv = es3.enter_context(nc.semaphore("fv"))
        s_fs = es3.enter_context(nc.semaphore("fs"))
        s_lp = es3.enter_context(nc.semaphore("lp"))
        s_t0 = es3.enter_context(nc.semaphore("t0"))
        s_t1 = es3.enter_context(nc.semaphore("t1"))
        s_o0 = es3.enter_context(nc.semaphore("o0"))
        s_o1 = es3.enter_context(nc.semaphore("o1"))
        ld, dv, ac, st = (Sem(x) for x in (s_ld, s_dv, s_ac, s_st))
        with nc.Block() as blk:
            @blk.vector
            def _(v):
                reg_consts(nc, v, cst, [EPS, -0.5])

            @blk.sync
            def _(s):
                s.dma_start(out=dps[:, :], in_=dpr[:, :]).then_inc(s_lp, 16)
                s.dma_start(out=nws[:, :], in_=nwr[:, :]).then_inc(s_lp, 16)
                for t in range(NTT):
                    if t >= 2:
                        s.wait_ge(dv.h, 3 * (t - 1))
                        s.wait_ge(ac.h, 3 * (t - 1))
                    p = (t % 2) * N
                    for src_, dst in ((y, ys), (xh, xs), (z, zs)):
                        s.dma_start(out=dst[:, p:p + N],
                                    in_=src_[t * 128:(t + 1) * 128, :]
                                    ).then_inc([s_t0, s_t1][t % 2], 16)

            @blk.scalar
            def _(sc):
                fs = Sem(s_fs)
                for t in range(NTT):
                    p = (t % 2) * N
                    sc.wait_ge([s_t0, s_t1][t % 2], 48 * (t // 2 + 1))
                    if t >= 1:
                        sc.wait_ge(dv.h, 3 * (t - 1) + 1)
                    sc.activation(zb[:, :], zs[:, p:p + N], AF.Silu
                                  ).then_inc(ac.h, 1)
                    sc.wait_ge(dv.h, 3 * t + 1)
                    for j in range(8):
                        ins_ = sc.activation(tr[:, j * 512:(j + 1) * 512],
                                             t1[:, j * 512:(j + 1) * 512],
                                             AF.Square,
                                             accum_out=stt[:, 8 + j:9 + j])
                    ins_.then_inc(ac.h, 1)
                    sc.wait_ge(dv.h, 3 * t + 2)
                    fence(sc, fs, sc.activation(stt[:, 2:3], stt[:, 1:2],
                                                AF.Ln, bias=EPS))
                    sc.activation(stt[:, 2:3], stt[:, 2:3], AF.Exp,
                                  scale=-0.5).then_inc(ac.h, 1)

            @blk.vector
            def _(v):
                fv = Sem(s_fv)
                v.wait_ge(s_lp, 32)
                for t in range(NTT):
                    p = (t % 2) * N
                    v.wait_ge([s_t0, s_t1][t % 2], 48 * (t // 2 + 1))
                    fence(v, fv, v.tensor_mul(t1[:, :], xs[:, p:p + N],
                                              dps[:, :]))
                    fence(v, fv, v.tensor_add(t1[:, :], t1[:, :],
                                              ys[:, p:p + N]))
                    v.wait_ge(ac.h, 3 * t + 1)
                    dv.inc(v.tensor_mul(t1[:, :], t1[:, :], zb[:, :]))
                    v.wait_ge(ac.h, 3 * t + 2)
                    fence(v, fv, v.reduce_sum(stt[:, 0:1], stt[:, 8:16],
                                              axis=AX.X))
                    dv.inc(v.tensor_scalar_mul(stt[:, 1:2], stt[:, 0:1],
                                               1.0 / N))
                    v.wait_ge(ac.h, 3 * t + 3)
                    if t >= 2:
                        v.wait_ge([s_o0, s_o1][t % 2], 16 * (t // 2))
                    fence(v, fv, v.tensor_scalar(ob[:, p:p + N], t1[:, :],
                                                 stt[:, 2:3], None, ALU.mult))
                    dv.inc(v.tensor_mul(ob[:, p:p + N], ob[:, p:p + N],
                                        nws[:, :]))

            @blk.gpsimd
            def _(g):
                for t in range(NTT):
                    g.wait_ge(dv.h, 3 * t + 3)
                    p = (t % 2) * N
                    g.dma_start(out=yf[t * 128:(t + 1) * 128, :],
                                in_=ob[:, p:p + N]).then_inc(
                                    [s_o0, s_o1][t % 2], 16)
                g.wait_ge(s_o0, 16 * 8)
                g.wait_ge(s_o1, 16 * 8)
    return nc


def build_fm_proj(name, n_i, wcast, dual, act):
    nc = bass.Bass("TRN2", target_bir_lowering=False)
    xT = nc.dram_tensor("xT", [D, TT], BF16, kind="ExternalInput")
    wdt = F32 if wcast else BF16
    w1 = nc.dram_tensor("w1", [D, n_i], wdt, kind="ExternalInput")
    w2 = nc.dram_tensor("w2", [D, n_i], wdt, kind="ExternalInput") if dual else None
    bvec = (nc.dram_tensor("bvec", [128, n_i // 128], F32,
                           kind="ExternalInput")
            if act == "gelu_bias" else None)
    mo = nc.dram_tensor("midT", [n_i, TT], BF16, kind="ExternalOutput")
    NIT = n_i // 128
    es4 = ExitStack()
    with es4:
        s_ld = es4.enter_context(nc.semaphore("ld"))
        s_wl = es4.enter_context(nc.semaphore("wl"))
        s_mm = es4.enter_context(nc.semaphore("mm"))
        s_ev = es4.enter_context(nc.semaphore("ev"))
        s_st = es4.enter_context(nc.semaphore("st"))
        s_ac = es4.enter_context(nc.semaphore("ac"))
        s_lx = es4.enter_context(nc.semaphore("lx"))
        s_lb = es4.enter_context(nc.semaphore("lb"))
        s_w0 = es4.enter_context(nc.semaphore("w0"))
        s_w1 = es4.enter_context(nc.semaphore("w1"))
        s_o0 = es4.enter_context(nc.semaphore("o0"))
        s_o1 = es4.enter_context(nc.semaphore("o1"))
        s_o2 = es4.enter_context(nc.semaphore("o2"))
        s_o3 = es4.enter_context(nc.semaphore("o3"))
        xTs = es4.enter_context(nc.sbuf_tensor("xTs", [128, KC * TT], BF16))
        w1b = es4.enter_context(nc.sbuf_tensor("w1b", [128, 2 * KC * 128], BF16))
        w2b = es4.enter_context(nc.sbuf_tensor("w2b", [128, 2 * KC * 128], BF16))
        bv = es4.enter_context(nc.sbuf_tensor("bv", [128, 64], F32))
        sg = es4.enter_context(nc.sbuf_tensor("sg", [128, 2 * 512], BF16))
        ob = es4.enter_context(nc.sbuf_tensor("ob", [128, 4 * 512], BF16))
        pg0 = es4.enter_context(nc.psum_tensor("pg0", [128, 512], F32))
        pg1 = es4.enter_context(nc.psum_tensor("pg1", [128, 512], F32))
        pu0 = es4.enter_context(nc.psum_tensor("pu0", [128, 512], F32))
        pu1 = es4.enter_context(nc.psum_tensor("pu1", [128, 512], F32))
        pg = [pg0, pg1]
        pu = [pu0, pu1]
        ld, wl, mm, ev, st = (Sem(x) for x in (s_ld, s_wl, s_mm, s_ev, s_st))
        NW = 2 if dual else 1
        with nc.Block() as blk:
            @blk.sync
            def _(s):
                s.dma_start(out=xTs.ap().rearrange("p (k t) -> p k t", t=TT),
                            in_=xT.ap().rearrange("(k p) t -> p k t", p=128)
                            ).then_inc(s_lx, 16)
                if bvec is not None:
                    s.dma_start(out=bv[:, 0:NIT], in_=bvec[:, :]
                                ).then_inc(s_lb, 16)

        def wload(g, it):
            b = (it % 2) * KC * 128
            g.dma_start(
                out=w1b[:, b:b + KC * 128].rearrange(
                    "p (k n) -> p k n", n=128),
                in_=w1[:, it * 128:(it + 1) * 128].rearrange(
                    "(k p) n -> p k n", p=128)
                ).then_inc([s_w0, s_w1][it % 2], 16)
            if dual:
                g.dma_start(
                    out=w2b[:, b:b + KC * 128].rearrange(
                        "p (k n) -> p k n", n=128),
                    in_=w2[:, it * 128:(it + 1) * 128].rearrange(
                        "(k p) n -> p k n", p=128)
                    ).then_inc([s_w0, s_w1][it % 2], 16)

        with nc.Block() as blk:
            @blk.gpsimd
            def _(g):
                wload(g, 0)
                wload(g, 1)
                for it in range(NIT):
                    for tc in range(4):
                        done = it * 4 + tc
                        g.wait_ge(ev.h, done + 1)
                        o = ob[:, (done % 4) * 512:(done % 4) * 512 + 512]
                        g.dma_start(out=mo[it * 128:(it + 1) * 128,
                                           tc * 512:(tc + 1) * 512],
                                    in_=o).then_inc(
                                        [s_o0, s_o1, s_o2, s_o3][done % 4], 16)
                    if it + 2 < NIT:
                        wload(g, it + 2)
                for q in range(4):
                    tot = NIT * 4
                    g.wait_ge([s_o0, s_o1, s_o2, s_o3][q],
                              16 * ((tot - q + 3) // 4))

            @blk.tensor
            def _(te):
                te.wait_ge(s_lx, 16)
                for it in range(NIT):
                    te.wait_ge([s_w0, s_w1][it % 2], (it // 2 + 1) * 16 * NW)
                    b = (it % 2) * KC * 128
                    for tc in range(4):
                        done = it * 4 + tc
                        if done >= 2:
                            if dual:
                                te.wait_ge(s_ac, done - 1)
                                te.wait_ge(ev.h, done - 1)
                            else:
                                te.wait_ge(ev.h, done - 1)
                        for k in range(KC):
                            ins = te.matmul(
                                pg[tc % 2][:, :],
                                w1b[:, b + k * 128:b + (k + 1) * 128],
                                xTs[:, k * TT + tc * 512:k * TT + (tc + 1) * 512],
                                start=(k == 0), stop=(k == KC - 1))
                        if dual:
                            for k in range(KC):
                                ins = te.matmul(
                                    pu[tc % 2][:, :],
                                    w2b[:, b + k * 128:b + (k + 1) * 128],
                                    xTs[:, k * TT + tc * 512:
                                        k * TT + (tc + 1) * 512],
                                    start=(k == 0), stop=(k == KC - 1))
                        mm.inc(ins)

            @blk.scalar
            def _(sc):
                for it in range(NIT):
                    for tc in range(4):
                        done = it * 4 + tc
                        o = ob[:, (done % 4) * 512:(done % 4) * 512 + 512]
                        if done >= 4:
                            sc.wait_ge([s_o0, s_o1, s_o2, s_o3][done % 4],
                                       16 * (done // 4))
                        if dual:
                            sc.wait_ge(mm.h, done + 1)
                            if done >= 2:
                                sc.wait_ge(ev.h, done - 1)
                            sc.activation(sg[:, (done % 2) * 512:
                                             (done % 2) * 512 + 512],
                                          pg[tc % 2][:, :], AF.Silu
                                          ).then_inc(s_ac, 1)
                        else:
                            sc.wait_ge(mm.h, done + 1)
                            sc.wait_ge(s_lb, 16)
                            ev.inc(sc.activation(o, pg[tc % 2][:, :],
                                                 AF.Gelu_apprx_tanh,
                                                 bias=bv[:, it:it + 1]))

            @blk.vector
            def _(v):
                if dual:
                    for it in range(NIT):
                        for tc in range(4):
                            done = it * 4 + tc
                            o = ob[:, (done % 4) * 512:(done % 4) * 512 + 512]
                            if done >= 4:
                                v.wait_ge([s_o0, s_o1, s_o2, s_o3][done % 4],
                                          16 * (done // 4))
                            v.wait_ge(s_ac, done + 1)
                            ev.inc(v.tensor_mul(
                                o, sg[:, (done % 2) * 512:
                                      (done % 2) * 512 + 512],
                                pu[tc % 2][:, :]))


    return nc


def build_router():
    nc = bass.Bass("TRN2", target_bir_lowering=False)
    xT = nc.dram_tensor("xT", [D, TT], BF16, kind="ExternalInput")
    wT = nc.dram_tensor("wT", [D, 16384], F32, kind="ExternalInput")
    brep = nc.dram_tensor("brep", [128, 16384], F32, kind="ExternalInput")
    co = nc.dram_tensor("c", [TT, 8], F32, kind="ExternalOutput")
    NCC = 32
    es5 = ExitStack()
    with es5:
        s_ld = es5.enter_context(nc.semaphore("ld"))
        s_wl = es5.enter_context(nc.semaphore("wl"))
        s_mm = es5.enter_context(nc.semaphore("mm"))
        s_ev = es5.enter_context(nc.semaphore("ev"))
        s_st = es5.enter_context(nc.semaphore("st"))
        s_pp = es5.enter_context(nc.semaphore("pp"))
        xTs = es5.enter_context(nc.sbuf_tensor("xTs", [128, KC * TT], BF16))
        wbuf = es5.enter_context(nc.sbuf_tensor("wbuf", [128, 2 * KC * 512], BF16))
        bb = es5.enter_context(nc.sbuf_tensor("bb", [128, 2 * 512], F32))
        lg = es5.enter_context(nc.sbuf_tensor("lg", [128, 2 * 512], F32))
        eq1 = es5.enter_context(nc.sbuf_tensor("eq1", [128, 512], F32))
        eq2 = es5.enter_context(nc.sbuf_tensor("eq2", [128, 512], F32))
        l2b = es5.enter_context(nc.sbuf_tensor("l2", [128, 512], F32))
        tmpb = es5.enter_context(nc.sbuf_tensor("tmp", [128, 512], F32))
        m1b = es5.enter_context(nc.sbuf_tensor("m1", [128, 64], F32))
        m2b = es5.enter_context(nc.sbuf_tensor("m2", [128, 64], F32))
        w1b = es5.enter_context(nc.sbuf_tensor("w1", [128, 64], F32))
        w2b = es5.enter_context(nc.sbuf_tensor("w2", [128, 64], F32))
        s_fv = es5.enter_context(nc.semaphore("fv"))
        s_lx = es5.enter_context(nc.semaphore("lx"))
        s_w0 = es5.enter_context(nc.semaphore("w0"))
        s_w1 = es5.enter_context(nc.semaphore("w1"))
        cacc = es5.enter_context(nc.sbuf_tensor("cacc", [128, NTT * 8], F32))
        cch = es5.enter_context(nc.sbuf_tensor("cch", [128, 16], F32))
        ps0 = es5.enter_context(nc.psum_tensor("ps0", [128, 512], F32))
        ps1 = es5.enter_context(nc.psum_tensor("ps1", [128, 512], F32))
        ps = [ps0, ps1]
        ld, wl, mm, ev, st, pp = (Sem(x) for x in
                                  (s_ld, s_wl, s_mm, s_ev, s_st, s_pp))
        with nc.Block() as blk:
            @blk.sync
            def _(s):
                s.dma_start(out=xTs.ap().rearrange("p (k t) -> p k t", t=TT),
                            in_=xT.ap().rearrange("(k p) t -> p k t", p=128)
                            ).then_inc(s_lx, 16)

            @blk.vector
            def _(v):
                v.memset(cacc[:, :], 0.0)

        with nc.Block() as blk:
            @blk.gpsimd
            def _(g):
                for i in range(NCC):
                    if i >= 2:
                        g.wait_ge(mm.h, (i - 1) * NTT)
                    b = (i % 2) * KC * 512
                    g.dma_start(out=wbuf[:, b:b + KC * 512].rearrange(
                                    "p (k n) -> p k n", n=512),
                                in_=wT[:, i * 512:(i + 1) * 512].rearrange(
                                    "(k p) n -> p k n", p=128)
                                ).then_inc([s_w0, s_w1][i % 2], 16)

            @blk.sync
            def _(s):
                for i in range(NCC):
                    if i >= 2:
                        s.wait_ge(ev.h, (i - 1) * NTT)
                    s.dma_start(out=bb[:, (i % 2) * 512:(i % 2) * 512 + 512],
                                in_=brep[:, i * 512:(i + 1) * 512]
                                ).then_inc([s_w0, s_w1][i % 2], 16)

            @blk.tensor
            def _(te):
                te.wait_ge(s_lx, 16)
                for i in range(NCC):
                    te.wait_ge([s_w0, s_w1][i % 2], (i // 2 + 1) * 32)
                    b = (i % 2) * KC * 512
                    for t in range(NTT):
                        done = i * NTT + t
                        if done >= 2:
                            te.wait_ge(ev.h, done - 1)
                        p = ps[t % 2]
                        for k in range(KC):
                            ins = te.matmul(
                                p[:, :],
                                xTs[:, k * TT + t * 128:k * TT + (t + 1) * 128],
                                wbuf[:, b + k * 512:b + (k + 1) * 512],
                                start=(k == 0), stop=(k == KC - 1))
                        mm.inc(ins)

            @blk.scalar
            def _(sc):
                for k in range(NCC * NTT):
                    sc.wait_ge(s_pp, 2 * k + 1)
                    sc.activation(w1b[:, :], w1b[:, :], AF.Sigmoid
                                  ).then_inc(s_pp, 1)

            @blk.vector
            def _(v):
                fv = Sem(s_fv)
                for i in range(NCC):
                    for t in range(NTT):
                        done = i * NTT + t
                        v.wait_ge(mm.h, done + 1)
                        p = ps[t % 2]
                        lgs = lg[:, (done % 2) * 512:(done % 2) * 512 + 512]
                        fence(v, ev, v.tensor_add(
                            lgs, p[:, :],
                            bb[:, (i % 2) * 512:(i % 2) * 512 + 512]))
                        lg3 = lgs.rearrange("q (o e) -> q o e", e=8)
                        m13 = m1b.ap().rearrange("q (o e) -> q o e", e=1)
                        fence(v, fv, v.tensor_reduce(m13, lg3, axis=AX.X,
                                                     op=ALU.max))
                        m1bc = bass.AP(m1b, 0, [[64, 128], [1, 64], [0, 8]])
                        fence(v, fv, v.tensor_tensor(eq1.ap().rearrange(
                            "q (o e) -> q o e", e=8), lg3, m1bc,
                            ALU.is_equal))
                        fence(v, fv, v.scalar_tensor_tensor(
                            l2b[:, :], eq1[:, :], NEG, lgs,
                            ALU.mult, ALU.add))
                        fence(v, fv, v.tensor_reduce(
                            m2b.ap().rearrange("q (o e) -> q o e", e=1),
                            l2b.ap().rearrange("q (o e) -> q o e", e=8),
                            axis=AX.X, op=ALU.max))
                        m2bc = bass.AP(m2b, 0, [[64, 128], [1, 64], [0, 8]])
                        fence(v, fv, v.tensor_tensor(eq2.ap().rearrange(
                            "q (o e) -> q o e", e=8), lg3, m2bc, ALU.is_equal))
                        pp.inc(v.tensor_sub(w1b[:, :], m1b[:, :], m2b[:, :]))
                        pp.n += 1
                        v.wait_ge(pp.h, pp.n)
                        fence(v, fv, v.tensor_scalar(
                            w2b[:, :], w1b[:, :], -1.0, 1.0,
                            ALU.mult, ALU.add))
                        w1bc = bass.AP(w1b, 0, [[64, 128], [0, 8], [1, 64]])
                        eq1v = bass.AP(eq1, 0, [[512, 128], [1, 8], [8, 64]])
                        tmp3 = tmpb.ap().rearrange("q (e o) -> q e o", o=64)
                        fence(v, fv, v.tensor_tensor(tmp3, eq1v, w1bc,
                                                     ALU.mult))
                        fence(v, fv, v.tensor_reduce(cch.ap().rearrange(
                            "q (e o) -> q e o", o=1)[:, 0:8, :],
                            tmp3, axis=AX.X, op=ALU.add))
                        w2bc = bass.AP(w2b, 0, [[64, 128], [0, 8], [1, 64]])
                        eq2v = bass.AP(eq2, 0, [[512, 128], [1, 8], [8, 64]])
                        fence(v, fv, v.tensor_tensor(tmp3, eq2v, w2bc,
                                                     ALU.mult))
                        fence(v, fv, v.tensor_reduce(cch.ap().rearrange(
                            "q (e o) -> q e o", o=1)[:, 8:16, :],
                            tmp3, axis=AX.X, op=ALU.add))
                        fence(v, fv, v.tensor_add(cch[:, 0:8], cch[:, 0:8],
                                                  cch[:, 8:16]))
                        v.tensor_add(cacc[:, t * 8:(t + 1) * 8],
                                     cacc[:, t * 8:(t + 1) * 8], cch[:, 0:8])

            @blk.gpsimd
            def _(g):
                pass

        with nc.Block() as blk:
            @blk.gpsimd
            def _(g):
                g.dma_start(out=co.ap().rearrange("(t p) e -> p t e", p=128),
                            in_=cacc.ap().rearrange("p (t e) -> p t e", e=8)
                            ).then_inc(st.h, 16)
                g.wait_ge(st.h, 16)
    return nc


def build_router_v2():
    NCC, NG = 32, 128
    nc = bass.Bass("TRN2", target_bir_lowering=False)
    xT = nc.dram_tensor("xT", [D, TT], BF16, kind="ExternalInput")
    wT = nc.dram_tensor("wT", [D, 16384], F32, kind="ExternalInput")
    brep = nc.dram_tensor("brep", [128, 16384], F32, kind="ExternalInput")
    co = nc.dram_tensor("c", [TT, 8], F32, kind="ExternalOutput")
    es = ExitStack()
    with es:
        s_lx = es.enter_context(nc.semaphore("lx"))
        s_w0 = es.enter_context(nc.semaphore("w0"))
        s_w1 = es.enter_context(nc.semaphore("w1"))
        s_mm = es.enter_context(nc.semaphore("mm"))
        s_ev = es.enter_context(nc.semaphore("ev"))
        s_pp = es.enter_context(nc.semaphore("pp"))
        s_sg = es.enter_context(nc.semaphore("sg"))
        s_bd = es.enter_context(nc.semaphore("bd"))
        s_fv = es.enter_context(nc.semaphore("fv"))
        s_st = es.enter_context(nc.semaphore("st"))
        xTs = es.enter_context(nc.sbuf_tensor("xTs", [128, KC * TT], BF16))
        wbuf = es.enter_context(nc.sbuf_tensor("wbuf", [128, 2 * KC * 512], BF16))
        bb = es.enter_context(nc.sbuf_tensor("bb", [128, 2 * 512], F32))
        lgs = es.enter_context(nc.sbuf_tensor("lgs", [128, 2 * 2048], F32))
        u2 = es.enter_context(nc.sbuf_tensor("u2", [128, 2 * 2048], F32))
        sgb = es.enter_context(nc.sbuf_tensor("sgb", [128, 2 * 2048], F32))
        eq1 = es.enter_context(nc.sbuf_tensor("eq1", [128, 2048], F32))
        l2b = es.enter_context(nc.sbuf_tensor("l2b", [128, 2048], F32))
        mask = es.enter_context(nc.sbuf_tensor("mask", [128, 2048], F32))
        m1b = es.enter_context(nc.sbuf_tensor("m1b", [128, 256], F32))
        m2b = es.enter_context(nc.sbuf_tensor("m2b", [128, 2 * 256], F32))
        Sb = es.enter_context(nc.sbuf_tensor("Sb", [128, 256], F32))
        cch = es.enter_context(nc.sbuf_tensor("cch", [128, 32], F32))
        cacc = es.enter_context(nc.sbuf_tensor("cacc", [128, NTT * 8], F32))
        psA = es.enter_context(nc.psum_tensor("psA", [128, 2048], F32))
        psB = es.enter_context(nc.psum_tensor("psB", [128, 2048], F32))
        ps = [psA, psB]
        mm, ev, pp, sg, bd, st = (Sem(x) for x in
                                  (s_mm, s_ev, s_pp, s_sg, s_bd, s_st))
        ws_ = [s_w0, s_w1]

        with nc.Block() as blk:
            @blk.sync
            def _(s):
                s.dma_start(out=xTs.ap().rearrange("p (k t) -> p k t", t=TT),
                            in_=xT.ap().rearrange("(k p) t -> p k t", p=128)
                            ).then_inc(s_lx, 16)
                for i in range(NCC):
                    if i >= 2:
                        s.wait_ge(s_ev, 4 * (i - 1))
                    s.dma_start(out=bb[:, (i % 2) * 512:(i % 2) * 512 + 512],
                                in_=brep[:, i * 512:(i + 1) * 512]
                                ).then_inc(ws_[i % 2], 16)

            @blk.gpsimd
            def _(g):
                for i in range(NCC):
                    if i >= 2:
                        g.wait_ge(s_mm, 16 * (i - 1))
                    b = (i % 2) * KC * 512
                    g.dma_start(
                        out=wbuf[:, b:b + KC * 512].rearrange(
                            "p (k n) -> p k n", n=512),
                        in_=wT[:, i * 512:(i + 1) * 512].rearrange(
                            "(k p) n -> p k n", p=128)
                        ).then_inc(ws_[i % 2], 16)
                g.wait_ge(s_bd, NG)
                g.dma_start(out=co.ap().rearrange("(t p) e -> p t e", p=128),
                            in_=cacc.ap().rearrange("p (t e) -> p t e", e=8)
                            ).then_inc(st.h, 16)
                g.wait_ge(st.h, 16)

            @blk.tensor
            def _(te):
                te.wait_ge(s_lx, 16)
                for i in range(NCC):
                    te.wait_ge(ws_[i % 2], 32 * (i // 2 + 1))
                    b = (i % 2) * KC * 512
                    for gq in range(4):
                        G = 4 * i + gq
                        if G >= 2:
                            te.wait_ge(s_ev, G - 1)
                        p = ps[G % 2]
                        for tc in range(4):
                            t = 4 * gq + tc
                            for k in range(KC):
                                ins = te.matmul(
                                    p[:, tc * 512:(tc + 1) * 512],
                                    xTs[:, k * TT + t * 128:
                                        k * TT + (t + 1) * 128],
                                    wbuf[:, b + k * 512:b + (k + 1) * 512],
                                    start=(k == 0), stop=(k == KC - 1))
                            mm.inc(ins)

            @blk.vector
            def _(v):
                fv = Sem(s_fv)
                v.memset(cacc[:, :], 0.0)

                def oe8(buf, off):
                    return buf[:, off:off + 2048].rearrange(
                        "q (oo e) -> q oo e", e=8)

                def bc8(buf, off, pstride):
                    return bass.AP(buf, off, [[pstride, 128], [1, 256],
                                              [0, 8]])

                def red1(buf, off):
                    return buf[:, off:off + 256].rearrange(
                        "q (o e) -> q o e", e=1)

                def stageB(g):
                    gq = g % 4
                    v.wait_ge(s_sg, g + 1)
                    q = (g % 2) * 2048
                    fence(v, fv, v.tensor_tensor(
                        oe8(mask, 0), oe8(lgs, q), bc8(m2b, (g % 2) * 256, 512),
                        ALU.is_ge))
                    fence(v, fv, v.tensor_mul(mask[:, :], mask[:, :],
                                              sgb[:, q:q + 2048]))
                    for tc in range(4):
                        ins = v.tensor_reduce(
                            cch[:, tc * 8:tc * 8 + 8].rearrange(
                                "q (e o) -> q e o", o=1),
                            bass.AP(mask, tc * 512, [[2048, 128], [1, 8],
                                                     [8, 64]]),
                            axis=AX.X, op=ALU.add)
                    fence(v, fv, ins)
                    fence(v, bd, v.tensor_add(cacc[:, gq * 32:(gq + 1) * 32],
                                              cacc[:, gq * 32:(gq + 1) * 32],
                                              cch[:, :]))

                for G in range(NG):
                    i = G // 4
                    v.wait_ge(ws_[i % 2], 32 * (i // 2 + 1))
                    v.wait_ge(s_mm, 4 * (G + 1))
                    q = (G % 2) * 2048
                    p = ps[G % 2]
                    bbv = bass.AP(bb, (i % 2) * 512, [[1024, 128], [0, 4],
                                                      [1, 512]])
                    pv = p[:, :].rearrange("q (tc n) -> q tc n", n=512)
                    ov = lgs[:, q:q + 2048].rearrange(
                        "q (tc n) -> q tc n", n=512)
                    fence(v, ev, v.tensor_tensor(ov, pv, bbv, ALU.add))
                    fence(v, fv, v.tensor_reduce(
                        red1(m1b, 0), oe8(lgs, q), axis=AX.X, op=ALU.max))
                    fence(v, fv, v.tensor_tensor(oe8(eq1, 0), oe8(lgs, q),
                                                 bc8(m1b, 0, 256),
                                                 ALU.is_equal))
                    fence(v, fv, v.scalar_tensor_tensor(
                        l2b[:, :], eq1[:, :], NEG, lgs[:, q:q + 2048],
                        ALU.mult, ALU.add))
                    fence(v, fv, v.tensor_reduce(
                        red1(m2b, (G % 2) * 256), oe8(l2b, 0), axis=AX.X,
                        op=ALU.max))
                    fence(v, fv, v.tensor_add(Sb[:, :], m1b[:, :],
                                              m2b[:, (G % 2) * 256:
                                                  (G % 2) * 256 + 256]))
                    pp.inc(v.scalar_tensor_tensor(
                        oe8(u2, q), oe8(lgs, q), 2.0, bc8(Sb, 0, 256),
                        ALU.mult, ALU.subtract))
                    if G >= 1:
                        stageB(G - 1)
                stageB(NG - 1)

            @blk.scalar
            def _(sc):
                for G in range(NG):
                    sc.wait_ge(s_pp, G + 1)
                    if G >= 2:
                        sc.wait_ge(s_bd, G - 1)
                    q = (G % 2) * 2048
                    sc.activation(sgb[:, q:q + 2048], u2[:, q:q + 2048],
                                  AF.Sigmoid).then_inc(s_sg, 1)
    return nc


def build_moe_mm():
    nc = bass.Bass("TRN2", target_bir_lowering=False)
    xT = nc.dram_tensor("xT", [D, TT], BF16, kind="ExternalInput")
    ewT = nc.dram_tensor("ewT", [NEXP, D, D], F32, kind="ExternalInput")
    eb = nc.dram_tensor("eb", [NEXP, D], F32, kind="ExternalInput")
    cin = nc.dram_tensor("c", [TT, 8], F32, kind="ExternalInput")
    cTin = nc.dram_tensor("cT", [8, TT], F32, kind="ExternalInput")
    mo = nc.dram_tensor("mm", [TT, D], F32, kind="ExternalOutput")
    es6 = ExitStack()
    with es6:
        s_ld = es6.enter_context(nc.semaphore("ld"))
        s_wl = es6.enter_context(nc.semaphore("wl"))
        s_mm = es6.enter_context(nc.semaphore("mm"))
        s_ev = es6.enter_context(nc.semaphore("ev"))
        s_ac = es6.enter_context(nc.semaphore("ac"))
        s_st = es6.enter_context(nc.semaphore("st"))
        s_w0 = es6.enter_context(nc.semaphore("w0"))
        s_w1 = es6.enter_context(nc.semaphore("w1"))
        s_fv = es6.enter_context(nc.semaphore("fvm"))
        s_lg = es6.enter_context(nc.semaphore("lg"))
        xTs = es6.enter_context(nc.sbuf_tensor("xTs", [128, KC * TT], BF16))
        wbuf = es6.enter_context(nc.sbuf_tensor("wbuf", [128, 2 * KC * 512], BF16))
        csb = es6.enter_context(nc.sbuf_tensor("csb", [128, NTT * 8], F32))
        cTb = es6.enter_context(nc.sbuf_tensor("cTb", [8, TT], BF16))
        ebb = es6.enter_context(nc.sbuf_tensor("ebb", [8, D], BF16))
        tmps = es6.enter_context(nc.sbuf_tensor("tmp", [128, 2 * 512], BF16))
        acc = es6.enter_context(nc.sbuf_tensor("acc", [128, NTT * 512], F32))
        ps0 = es6.enter_context(nc.psum_tensor("ps0", [128, 512], F32))
        ps1 = es6.enter_context(nc.psum_tensor("ps1", [128, 512], F32))
        pb2 = es6.enter_context(nc.psum_tensor("pb2", [128, 1024], F32))
        ps = [ps0, ps1]
        ld, wl, mm, ev, ac, st = (Sem(x) for x in
                                  (s_ld, s_wl, s_mm, s_ev, s_ac, s_st))
        with nc.Block() as blk:
            @blk.sync
            def _(s):
                s.dma_start(out=xTs.ap().rearrange("p (k t) -> p k t", t=TT),
                            in_=xT.ap().rearrange("(k p) t -> p k t", p=128)
                            ).then_inc(ld.h, 16)
                s.dma_start(out=csb.ap().rearrange("p (t e) -> p t e", e=8),
                            in_=cin.ap().rearrange("(t p) e -> p t e", p=128)
                            ).then_inc(ld.h, 16)
                ld.n += 32

            @blk.gpsimd
            def _(g):
                g.dma_start(out=cTb[:, :], in_=cTin[:, :]).then_inc(s_lg, 16)
                g.dma_start(out=ebb[:, :], in_=eb[:, :]).then_inc(s_lg, 16)

        for cc in range(4):
            c0 = cc * 512
            with nc.Block() as blk:
                @blk.gpsimd
                def _(g):
                    for e in range(NEXP):
                        if e >= 2:
                            g.wait_ge(mm.h, mm.n + (e - 1) * NTT)
                        b = (e % 2) * KC * 512
                        g.dma_start(
                            out=wbuf[:, b:b + KC * 512].rearrange(
                                "p (k n) -> p k n", n=512),
                            in_=ewT[e, :, c0:c0 + 512].rearrange(
                                "(k p) n -> p k n", p=128)
                            ).then_inc([s_w0, s_w1][e % 2], 16)
                        wl.n += 16

                @blk.tensor
                def _(te):
                    te.wait_ge(ld.h, 32)
                    te.wait_ge(s_lg, 32)
                    for e in range(NEXP):
                        te.wait_ge([s_w0, s_w1][e % 2],
                                   16 * (cc * 4 + e // 2 + 1))
                        b = (e % 2) * KC * 512
                        for t in range(NTT):
                            done = e * NTT + t
                            if done >= 2:
                                te.wait_ge(ac.h, ac.n + done - 1)
                            p = ps[t % 2]
                            for k in range(KC):
                                ins = te.matmul(
                                    p[:, :],
                                    xTs[:, k * TT + t * 128:
                                        k * TT + (t + 1) * 128],
                                    wbuf[:, b + k * 512:b + (k + 1) * 512],
                                    start=(k == 0), stop=(k == KC - 1))
                            mm.inc(ins)
                    te.wait_ge(ev.h, ev.n + NEXP * NTT)
                    for t in range(NTT):
                        if t >= 2:
                            te.wait_ge(ev.h, ev.n + NEXP * NTT + t - 1)
                        mm.inc(te.matmul(pb2[:, (t % 2) * 512:(t % 2) * 512 + 512],
                                         cTb[:, t * 128:(t + 1) * 128],
                                         ebb[:, c0:c0 + 512],
                                         start=True, stop=True))

                @blk.scalar
                def _(sc):
                    mm0 = mm.n - NEXP * NTT - NTT
                    ev0 = ev.n
                    for e in range(NEXP):
                        for t in range(NTT):
                            done = e * NTT + t
                            sc.wait_ge(mm.h, mm0 + done + 1)
                            if done >= 2:
                                sc.wait_ge(ev.h, ev0 + done - 1)
                            p = ps[t % 2]
                            tb = tmps[:, (done % 2) * 512:(done % 2) * 512 + 512]
                            ac.inc(sc.activation(tb, p[:, :], AF.Copy,
                                                 scale=csb[:, t * 8 + e:
                                                           t * 8 + e + 1]))

                @blk.vector
                def _(v):
                    if cc >= 1:
                        v.wait_ge(st.h, st.n)
                    mm0 = mm.n - NEXP * NTT - NTT
                    ac0 = ac.n - NEXP * NTT
                    for e in range(NEXP):
                        for t in range(NTT):
                            done = e * NTT + t
                            v.wait_ge(ac.h, ac0 + done + 1)
                            tb = tmps[:, (done % 2) * 512:(done % 2) * 512 + 512]
                            a = acc[:, t * 512:(t + 1) * 512]
                            if e == 0:
                                ins = v.tensor_copy(a, tb)
                            else:
                                ins = v.tensor_add(a, a, tb)
                            fence(v, ev, ins)
                    for t in range(NTT):
                        v.wait_ge(mm.h, mm0 + NEXP * NTT + t + 1)
                        a = acc[:, t * 512:(t + 1) * 512]
                        fence(v, ev, v.tensor_add(
                            a, a, pb2[:, (t % 2) * 512:(t % 2) * 512 + 512]))

                @blk.gpsimd
                def _(g):
                    ev0 = ev.n - NTT
                    for t in range(NTT):
                        g.wait_ge(ev.h, ev0 + t + 1)
                        g.dma_start(out=mo[t * 128:(t + 1) * 128, c0:c0 + 512],
                                    in_=acc[:, t * 512:(t + 1) * 512]
                                    ).then_inc(st.h, 16)
                        st.n += 16
        with nc.Block() as blk:
            @blk.gpsimd
            def _(g):
                g.wait_ge(st.h, st.n)
    return nc


def build_tail_mm(name, ks, wcast, with_bias):
    nc = bass.Bass("TRN2", target_bir_lowering=False)
    xTs_d = [nc.dram_tensor(f"xT{i}", [k, TT], BF16, kind="ExternalInput")
             for i, k in enumerate(ks)]
    wTs_d = [nc.dram_tensor(f"wT{i}", [k, D], F32 if wcast else BF16,
                            kind="ExternalInput") for i, k in enumerate(ks)]
    bias = (nc.dram_tensor("bias", [128, D], F32, kind="ExternalInput")
            if with_bias else None)
    mo = nc.dram_tensor("mm", [TT, D], F32, kind="ExternalOutput")
    kcs = [k // 128 for k in ks]
    es7 = ExitStack()
    with es7:
        s_ld = es7.enter_context(nc.semaphore("ld"))
        s_wl = es7.enter_context(nc.semaphore("wl"))
        s_mm = es7.enter_context(nc.semaphore("mm"))
        s_ev = es7.enter_context(nc.semaphore("ev"))
        s_st = es7.enter_context(nc.semaphore("st"))
        s_x0 = es7.enter_context(nc.semaphore("x0"))
        s_x1 = es7.enter_context(nc.semaphore("x1"))
        s_o0 = es7.enter_context(nc.semaphore("so0"))
        s_o1 = es7.enter_context(nc.semaphore("so1"))
        s_o2 = es7.enter_context(nc.semaphore("so2"))
        s_o3 = es7.enter_context(nc.semaphore("so3"))
        w0 = es7.enter_context(nc.sbuf_tensor("w0", [128, kcs[0] * 512], BF16))
        w1 = es7.enter_context((nc.sbuf_tensor("w1", [128, kcs[1] * 512], BF16) if len(ks) > 1 else nc.sbuf_tensor("w1", [128, 16], BF16)))
        x0 = es7.enter_context(nc.sbuf_tensor("x0", [128, 2 * kcs[0] * 128], BF16))
        x1 = es7.enter_context((nc.sbuf_tensor("x1", [128, 2 * kcs[1] * 128], BF16) if len(ks) > 1 else nc.sbuf_tensor("x1", [128, 16], BF16)))
        bb = es7.enter_context(nc.sbuf_tensor("bb", [128, D], F32))
        ob = es7.enter_context(nc.sbuf_tensor("ob", [128, 4 * 512], F32))
        ps0 = es7.enter_context(nc.psum_tensor("ps0", [128, 512], F32))
        ps1 = es7.enter_context(nc.psum_tensor("ps1", [128, 512], F32))
        ps2 = es7.enter_context(nc.psum_tensor("ps2", [128, 512], F32))
        ps3 = es7.enter_context(nc.psum_tensor("ps3", [128, 512], F32))
        ps = [ps0, ps1, ps2, ps3]
        wb = [w0, w1]
        xb = [x0, x1]
        ld, wl, mm, ev, st = (Sem(x) for x in (s_ld, s_wl, s_mm, s_ev, s_st))
        if with_bias:
            with nc.Block() as blk:
                @blk.sync
                def _(s):
                    s.dma_start(out=bb[:, :], in_=bias[:, :]).then_inc(ld.h, 16)
                    ld.n += 16
        for cc in range(4):
            c0 = cc * 512
            with nc.Block() as blk:
                @blk.gpsimd
                def _(g):
                    for i, k in enumerate(ks):
                        g.dma_start(
                            out=wb[i].ap().rearrange("p (k n) -> p k n", n=512),
                            in_=wTs_d[i][:, c0:c0 + 512].rearrange(
                                "(k p) n -> p k n", p=128)).then_inc(wl.h, 16)
                        wl.n += 16

                @blk.sync
                def _(s):
                    for t in range(NTT):
                        if t >= 2:
                            s.wait_ge(mm.h, mm.n + t - 1)
                        for i, k in enumerate(ks):
                            b = (t % 2) * kcs[i] * 128
                            s.dma_start(
                                out=xb[i][:, b:b + kcs[i] * 128].rearrange(
                                    "p (k n) -> p k n", n=128),
                                in_=xTs_d[i][:, t * 128:(t + 1) * 128].rearrange(
                                    "(k p) n -> p k n", p=128)
                                ).then_inc([s_x0, s_x1][t % 2], 16)
                            ld.n += 16

                @blk.tensor
                def _(te):
                    te.wait_ge(wl.h, wl.n)
                    mm0 = mm.n
                    xc0 = cc * 8 * len(ks)
                    for t in range(NTT):
                        te.wait_ge([s_x0, s_x1][t % 2],
                                   16 * len(ks) * (cc * 8 + t // 2 + 1))
                        if t >= 4:
                            te.wait_ge(ev.h, ev.n + t - 3)
                        p = ps[t % 4]
                        first = True
                        for i, k in enumerate(ks):
                            b = (t % 2) * kcs[i] * 128
                            for kk in range(kcs[i]):
                                ins = te.matmul(
                                    p[:, :],
                                    xb[i][:, b + kk * 128:b + (kk + 1) * 128],
                                    wb[i][:, kk * 512:(kk + 1) * 512],
                                    start=first,
                                    stop=(i == len(ks) - 1 and kk == kcs[i] - 1))
                                first = False
                        mm.inc(ins)

                @blk.vector
                def _(v):
                    mm0 = mm.n - NTT
                    for t in range(NTT):
                        v.wait_ge(mm.h, mm0 + t + 1)
                        if t >= 4:
                            v.wait_ge([s_o0, s_o1, s_o2, s_o3][t % 4],
                                      16 * (cc * 4 + t // 4))
                        o = ob[:, (t % 4) * 512:(t % 4) * 512 + 512]
                        if with_bias:
                            v.wait_ge(ld.h, 16)
                            ev.inc(v.tensor_add(o, ps[t % 4][:, :],
                                                bb[:, c0:c0 + 512]))
                        else:
                            ev.inc(v.tensor_copy(o, ps[t % 4][:, :]))

                @blk.gpsimd
                def _(g):
                    ev0 = ev.n - NTT
                    for t in range(NTT):
                        g.wait_ge(ev.h, ev0 + t + 1)
                        o = ob[:, (t % 4) * 512:(t % 4) * 512 + 512]
                        g.dma_start(out=mo[t * 128:(t + 1) * 128, c0:c0 + 512],
                                    in_=o).then_inc(
                                        [s_o0, s_o1, s_o2, s_o3][t % 4], 16)
        with nc.Block() as blk:
            @blk.gpsimd
            def _(g):
                for q in range(4):
                    g.wait_ge([s_o0, s_o1, s_o2, s_o3][q], 16 * 16)
    return nc


def build_wbuild():
    nc = bass.Bass("TRN2", target_bir_lowering=False)
    gsl = nc.dram_tensor("gsl", [256, GINTER], I32, kind="ExternalInput")
    usl = nc.dram_tensor("usl", [256, GINTER], I32, kind="ExternalInput")
    dsl = nc.dram_tensor("dsl", [768, D], I32, kind="ExternalInput")
    lut = nc.dram_tensor("lut", [1, 16], F32, kind="ExternalInput")
    go = nc.dram_tensor("go", [256, GINTER], BF16, kind="ExternalOutput")
    uo = nc.dram_tensor("uo", [256, GINTER], BF16, kind="ExternalOutput")
    do = nc.dram_tensor("do", [768, D], BF16, kind="ExternalOutput")
    tiles = ([(gsl, go, r, GINTER) for r in range(2)]
             + [(usl, uo, r, GINTER) for r in range(2)]
             + [(dsl, do, r, D) for r in range(6)])
    es8 = ExitStack()
    with es8:
        s_ld = es8.enter_context(nc.semaphore("ld"))
        s_ev = es8.enter_context(nc.semaphore("ev"))
        s_st = es8.enter_context(nc.semaphore("st"))
        ib = es8.enter_context(nc.sbuf_tensor("ib", [128, 2 * GINTER], I32))
        xbf = es8.enter_context(nc.sbuf_tensor("xb", [128, GINTER], BF16))
        tb = es8.enter_context(nc.sbuf_tensor("tb", [128, GINTER], BF16))
        ac = es8.enter_context(nc.sbuf_tensor("ac", [128, 2 * GINTER], BF16))
        luts = es8.enter_context(nc.sbuf_tensor("luts", [128, 16], F32))
        ld, ev, st = (Sem(x) for x in (s_ld, s_ev, s_st))
        s_fw = es8.enter_context(nc.semaphore("fw"))
        s_i0 = es8.enter_context(nc.semaphore("i0"))
        s_i1 = es8.enter_context(nc.semaphore("i1"))
        s_o0 = es8.enter_context(nc.semaphore("o0"))
        s_o1 = es8.enter_context(nc.semaphore("o1"))
        with nc.Block() as blk:
            @blk.sync
            def _(s):
                s.dma_start(out=luts[:, :],
                            in_=bass.AP(lut, 0, [[0, 128], [1, 16]])
                            ).then_inc(ld.h, 16)
                for j, (src, dst, r, n) in enumerate(tiles):
                    if j >= 2:
                        s.wait_ge(ev.h, j - 1)
                    s.dma_start(out=ib[:, (j % 2) * GINTER:(j % 2) * GINTER + n],
                                in_=src[r * 128:(r + 1) * 128, :]
                                ).then_inc([s_i0, s_i1][j % 2], 16)

            @blk.vector
            def _(v):
                fw = Sem(s_fw)
                v.wait_ge(ld.h, 16)
                for j, (src, dst, r, n) in enumerate(tiles):
                    v.wait_ge([s_i0, s_i1][j % 2], 16 * (j // 2 + 1))
                    if j >= 2:
                        v.wait_ge([s_o0, s_o1][j % 2], 16 * (j // 2))
                    a = ac[:, (j % 2) * GINTER:(j % 2) * GINTER + n]
                    fence(v, fw, v.tensor_copy(
                        xbf[:, 0:n],
                        ib[:, (j % 2) * GINTER:(j % 2) * GINTER + n]))
                    for k in range(16):
                        fence(v, fw, v.tensor_scalar(
                            tb[:, 0:n], xbf[:, 0:n], float(k),
                            luts[:, k:k + 1], ALU.is_equal, ALU.mult))
                        if k == 0:
                            fence(v, fw, v.tensor_copy(a, tb[:, 0:n]))
                        elif k == 15:
                            ev.inc(v.tensor_add(a, a, tb[:, 0:n]))
                        else:
                            fence(v, fw, v.tensor_add(a, a, tb[:, 0:n]))

            @blk.gpsimd
            def _(g):
                for j, (src, dst, r, n) in enumerate(tiles):
                    g.wait_ge(ev.h, j + 1)
                    g.dma_start(out=dst[r * 128:(r + 1) * 128, :],
                                in_=ac[:, (j % 2) * GINTER:(j % 2) * GINTER + n]
                                ).then_inc([s_o0, s_o1][j % 2], 16)
                g.wait_ge(s_o0, 16 * 5)
                g.wait_ge(s_o1, 16 * 5)
    return nc


def build_scan():
    nc = bass.Bass("TRN2", target_bir_lowering=False)
    NU, NCH = 32, 32
    dtin = nc.dram_tensor("dtin", [NU, SEQ], F32, kind="ExternalInput")
    alog = nc.dram_tensor("alog", [NU, 1], F32, kind="ExternalInput")
    xt = nc.dram_tensor("xt", [NU, SEQ, HDIM], BF16, kind="ExternalInput")
    bt = nc.dram_tensor("bt", [NU, SEQ, HDIM], BF16, kind="ExternalInput")
    bfm = nc.dram_tensor("bfm", [NU, HDIM, SEQ], BF16, kind="ExternalInput")
    cfm = nc.dram_tensor("cfm", [NU, HDIM, SEQ], BF16, kind="ExternalInput")
    y = nc.dram_tensor("y", [NU, SEQ, HDIM], BF16, kind="ExternalOutput")
    lsc = nc.dram_tensor("lsc", [NU, SEQ], F32, kind="Internal")
    esc = nc.dram_tensor("esc", [NU, NCH], F32, kind="Internal")

    es9 = ExitStack()
    with es9:
        s_ld = es9.enter_context(nc.semaphore("ld"))
        s_pe = es9.enter_context(nc.semaphore("pe"))
        s_dv = es9.enter_context(nc.semaphore("dv"))
        s_ac = es9.enter_context(nc.semaphore("ac"))
        s_st = es9.enter_context(nc.semaphore("st"))
        s_pp = es9.enter_context(nc.semaphore("pp"))
        s_fv = es9.enter_context(nc.semaphore("fv"))
        s_fv2 = es9.enter_context(nc.semaphore("fv2"))
        s_lb0 = es9.enter_context(nc.semaphore("lb0"))
        s_lb1 = es9.enter_context(nc.semaphore("lb1"))
        s_y0 = es9.enter_context(nc.semaphore("y0"))
        s_y1 = es9.enter_context(nc.semaphore("y1"))
        dts = es9.enter_context(nc.sbuf_tensor("dts", [32, SEQ], F32))
        als = es9.enter_context(nc.sbuf_tensor("als", [32, 1], F32))
        av = es9.enter_context(nc.sbuf_tensor("av", [32, 1], F32))
        dta = es9.enter_context(nc.sbuf_tensor("dta", [32, SEQ], F32))
        Lp = es9.enter_context(nc.sbuf_tensor("Lp", [32, SEQ], F32))
        w2p = es9.enter_context(nc.sbuf_tensor("w2p", [32, SEQ], F32))
        esb = es9.enter_context(nc.sbuf_tensor("esb", [32, NCH], F32))
        easb = es9.enter_context(nc.sbuf_tensor("easb", [32, NCH], F32))
        arep = es9.enter_context(nc.sbuf_tensor("arep", [64, NU * NCH], F32))
        arow = es9.enter_context(nc.sbuf_tensor("arow", [1, NU * NCH], F32))
        LTp = es9.enter_context(nc.sbuf_tensor("LTp", [128, NCH * 32], F32))
        eLTp = es9.enter_context(nc.sbuf_tensor("eLTp", [128, NCH * 32], F32))
        w2Tp = es9.enter_context(nc.sbuf_tensor("w2Tp", [128, NCH * 32], F32))
        dtTp = es9.enter_context(nc.sbuf_tensor("dtTp", [128, NCH * 32], F32))
        Xall = es9.enter_context(nc.sbuf_tensor("Xall", [128, 2 * NU * 64], BF16))
        Ball = es9.enter_context(nc.sbuf_tensor("Ball", [128, 2 * NU * 64], BF16))
        BFall = es9.enter_context(nc.sbuf_tensor("BFall", [64, 2 * NU * 128], BF16))
        CFall = es9.enter_context(nc.sbuf_tensor("CFall", [64, 2 * NU * 128], BF16))
        Lrow = es9.enter_context(nc.sbuf_tensor("Lrow", [1, NU * 128], F32))
        dl = es9.enter_context(nc.sbuf_tensor("dl", [128, 2 * 128], F32))
        dbf = es9.enter_context(nc.sbuf_tensor("dbf", [128, 2 * 128], BF16))
        sbf = es9.enter_context(nc.sbuf_tensor("sbf", [128, 2 * 128], BF16))
        b2 = es9.enter_context(nc.sbuf_tensor("b2", [128, 2 * 64], BF16))
        ty = es9.enter_context(nc.sbuf_tensor("ty", [128, 2 * 64], F32))
        ybuf = es9.enter_context(nc.sbuf_tensor("ybuf", [128, 2 * NU * 64], BF16))
        stf = es9.enter_context(nc.sbuf_tensor("stf", [64, NU * 64], F32))
        stb = es9.enter_context(nc.sbuf_tensor("stb", [64, NU * 64], BF16))
        tri01 = es9.enter_context(nc.sbuf_tensor("tri01", [128, 128], BF16))
        identF = es9.enter_context(nc.sbuf_tensor("identF", [128, 128], F32))
        ones1f = es9.enter_context(nc.sbuf_tensor("ones1f", [1, 128], F32))
        iot = es9.enter_context(nc.sbuf_tensor("iot", [128, 128], F32))
        ld, pe, dv, ac, st, pp = (Sem(x) for x in
                                  (s_ld, s_pe, s_dv, s_ac, s_st, s_pp))
        with nc.psum_tensor("pA", [128, 128], F32) as pA, \
             nc.psum_tensor("pAr", [64, 1024], F32) as pAr:
            with nc.Block() as blk:
                @blk.gpsimd
                def _(g):
                    g.iota(iot[:, :], [[1, 128]], channel_multiplier=-1,
                           allow_small_or_imprecise_dtypes=True)
                    g.memset(ones1f[:, :], 1.0)

            with nc.Block() as blk:
                @blk.sync
                def _(s):
                    s.dma_start(out=dts[:, :], in_=dtin[:, :]).then_inc(ld.h, 16)
                    s.dma_start(out=als[:, :], in_=alog[:, :]).then_inc(ld.h, 16)
                    ld.n += 32

                @blk.scalar
                def _(sc):
                    sc.wait_ge(ld.h, 32)
                    sc.activation(av[:, :], als[:, :], AF.Exp
                                  ).then_inc(pp.h, 1)
                    sc.wait_ge(pp.h, 2)
                    sc.activation(easb[:, :], esb[:, :], AF.Exp)
                    sc.activation(w2p[:, :], w2p[:, :], AF.Exp,
                                  scale=-1.0).then_inc(pp.h, 1)

                @blk.vector
                def _(v):
                    v.tensor_scalar(identF[:, :], iot[:, :], 0.0, 1.0,
                                    ALU.is_equal, ALU.mult)
                    v.tensor_scalar(tri01[:, :], iot[:, :], 0.0, 1.0,
                                    ALU.is_ge, ALU.mult)
                    v.memset(stf[:, :], 0.0)
                    v.memset(stb[:, :], 0.0)
                    fv = Sem(s_fv)
                    pp.n += 1
                    v.wait_ge(pp.h, pp.n)
                    fence(v, fv, v.tensor_scalar_mul(av[:, :], av[:, :],
                                                     -1.0))
                    fence(v, fv, v.tensor_scalar(dta[:, :], dts[:, :],
                                                 av[:, 0:1], None, ALU.mult))
                    ins3 = None
                    for c in range(NCH):
                        sl = slice(c * 128, (c + 1) * 128)
                        ins3 = v.tensor_tensor_scan(Lp[:, sl], dta[:, sl],
                                                    dta[:, sl], 0.0, ALU.add,
                                                    ALU.bypass)
                    fence(v, fv, ins3)
                    fence(v, fv, v.tensor_copy(
                        esb[:, :], bass.AP(Lp, 127, [[SEQ, 32], [128, NCH]])))
                    ins2 = None
                    for c in range(NCH):
                        sl = slice(c * 128, (c + 1) * 128)
                        ins2 = v.tensor_scalar(w2p[:, sl], Lp[:, sl],
                                               esb[:, c:c + 1], None,
                                               ALU.subtract)
                    pp.inc(ins2)
                    pp.n += 1
                    v.wait_ge(pp.h, pp.n)
                    dv.inc(v.tensor_mul(w2p[:, :], w2p[:, :], dts[:, :]))

            with nc.Block() as blk:
                @blk.sync
                def _(s):
                    s.wait_ge(dv.h, dv.n)
                    s.dma_start(out=esc[:, :], in_=easb[:, :]).then_inc(ld.h, 16)
                    s.dma_start(out=lsc[:, :], in_=Lp[:, :]).then_inc(ld.h, 16)
                    ld.n += 32
                    s.wait_ge(ld.h, ld.n)
                    s.dma_start(out=arow[:, :],
                                in_=bass.AP(esc, 0, [[NU * NCH, 1],
                                                     [1, NU * NCH]])
                                ).then_inc(ld.h, 16)
                    ld.n += 16

                @blk.tensor
                def _(te):
                    te.wait_ge(ld.h, ld.n)
                    for j in range(2):
                        pe.inc(te.matmul(pAr[:, j * 512:(j + 1) * 512],
                                         ones1f[0:1, 0:64],
                                         arow[0:1, j * 512:(j + 1) * 512],
                                         start=True, stop=True))

                @blk.vector
                def _(v):
                    v.wait_ge(pe.h, pe.n)
                    dv.inc(v.tensor_copy(arep[:, :], pAr[0:64, :]))

            with nc.Block() as blk:
                @blk.tensor
                def _(te):
                    for c in range(NCH):
                        if c >= 1:
                            te.wait_ge(dv.h, dv.n + c)
                        sl = slice(c * 128, (c + 1) * 128)
                        te.transpose(pA[:, 0:32], Lp[0:32, sl],
                                     identF[0:32, 0:32])
                        te.transpose(pA[:, 64:96], w2p[0:32, sl],
                                     identF[0:32, 0:32])
                        pe.inc(te.transpose(pA[:, 96:128], dts[0:32, sl],
                                            identF[0:32, 0:32]))

                @blk.vector
                def _(v):
                    pe0 = pe.n - NCH
                    dv0 = dv.n
                    for c in range(NCH):
                        v.wait_ge(pe.h, pe0 + c + 1)
                        cs = slice(c * 32, (c + 1) * 32)
                        v.tensor_copy(LTp[:, cs], pA[:, 0:32])
                        v.tensor_copy(w2Tp[:, cs], pA[:, 64:96])
                        dv.inc(v.tensor_copy(dtTp[:, cs], pA[:, 96:128]))

                @blk.scalar
                def _(sc):
                    sc.wait_ge(dv.h, dv.n)
                    sc.activation(eLTp[:, :], LTp[:, :], AF.Exp)

        es10 = ExitStack()
        with es10:
            plbc = es10.enter_context(nc.psum_tensor("plbc", [128, 1024], F32))
            pS0 = es10.enter_context(nc.psum_tensor("pS0", [128, 128], F32))
            pS1 = es10.enter_context(nc.psum_tensor("pS1", [128, 128], F32))
            pSL0 = es10.enter_context(nc.psum_tensor("pSL0", [64, 64], F32))
            pSL1 = es10.enter_context(nc.psum_tensor("pSL1", [64, 64], F32))
            pY0 = es10.enter_context(nc.psum_tensor("pY0", [128, 128], F32))
            pY1 = es10.enter_context(nc.psum_tensor("pY1", [128, 128], F32))
            pS = [pS0, pS1]
            pSL = [pSL0, pSL1]
            pY = [pY0, pY1]
            ld0 = ld.n
            pe0 = pe.n
            dv0 = dv.n
            ac0 = ac.n
            with nc.Block() as blk:
                @blk.sync
                def _(s):
                    for c in range(NCH):
                        if c >= 2:
                            s.wait_ge(pe.h, pe0 + 96 * (c - 1))
                            s.wait_ge(dv.h, dv0 + 128 * (c - 1))
                        b = c % 2
                        sl = slice(c * 128, (c + 1) * 128)
                        s.dma_start(
                            out=Xall[:, b * NU * 64:(b + 1) * NU * 64].rearrange(
                                "p (u d) -> p u d", d=64),
                            in_=xt[:, sl, :].rearrange("u s d -> s u d")
                            ).then_inc([s_lb0, s_lb1][c % 2], 16)
                        s.dma_start(
                            out=Ball[:, b * NU * 64:(b + 1) * NU * 64].rearrange(
                                "p (u d) -> p u d", d=64),
                            in_=bt[:, sl, :].rearrange("u s d -> s u d")
                            ).then_inc([s_lb0, s_lb1][c % 2], 16)
                        s.dma_start(
                            out=BFall[:, b * NU * 128:(b + 1) * NU * 128
                                      ].rearrange("p (u t) -> p u t", t=128),
                            in_=bfm[:, :, sl].rearrange("u n s -> n u s")
                            ).then_inc([s_lb0, s_lb1][c % 2], 16)
                        s.dma_start(
                            out=CFall[:, b * NU * 128:(b + 1) * NU * 128
                                      ].rearrange("p (u t) -> p u t", t=128),
                            in_=cfm[:, :, sl].rearrange("u n s -> n u s")
                            ).then_inc([s_lb0, s_lb1][c % 2], 16)
                        if c >= 1:
                            s.wait_ge(pe.h, pe0 + 96 * c)
                        s.dma_start(
                            out=Lrow[0:1, :].rearrange(
                                "p (u t) -> p u t", t=128),
                            in_=bass.AP(lsc, c * 128,
                                        [[1, 1], [SEQ, NU], [1, 128]])
                            ).then_inc([s_lb0, s_lb1][c % 2], 16)

                @blk.tensor
                def _(te):
                    for c in range(NCH):
                        b = c % 2
                        te.wait_ge([s_lb0, s_lb1][c % 2], 80 * (c // 2 + 1))
                        for g in range(8):
                            G = c * 8 + g
                            if G >= 2:
                                te.wait_ge(dv.h,
                                           dv0 + 4 * ((G - 2) * 4 + 3) + 1)
                            te.matmul(plbc[:, (G % 2) * 512:(G % 2) * 512 + 512],
                                      ones1f[0:1, :],
                                      Lrow[0:1, g * 512:(g + 1) * 512],
                                      start=True, stop=True)
                            for ui in range(4):
                                u = g * 4 + ui
                                idx = c * 32 + u
                                r = idx % 2
                                if idx >= 2:
                                    te.wait_ge(dv.h, dv0 + 4 * (idx - 2) + 2)
                                pe.inc(te.matmul(
                                    pS[r][:, :],
                                    BFall[:, b * NU * 128 + u * 128:
                                          b * NU * 128 + (u + 1) * 128],
                                    CFall[:, b * NU * 128 + u * 128:
                                          b * NU * 128 + (u + 1) * 128],
                                    start=True, stop=True))
                                te.wait_ge(dv.h, dv0 + 4 * idx + 2)
                                te.matmul(
                                    pY[r][:, 0:64],
                                    sbf[:, r * 128:(r + 1) * 128],
                                    Xall[:, b * NU * 64 + u * 64:
                                         b * NU * 64 + (u + 1) * 64],
                                    start=True, stop=True)
                                pe.inc(te.matmul(
                                    pY[r][:, 64:128],
                                    CFall[:, b * NU * 128 + u * 128:
                                          b * NU * 128 + (u + 1) * 128],
                                    stb[:, u * 64:(u + 1) * 64],
                                    start=True, stop=True))
                                pe.inc(te.matmul(
                                    pSL[r][:, :],
                                    b2[:, r * 64:(r + 1) * 64],
                                    Xall[:, b * NU * 64 + u * 64:
                                         b * NU * 64 + (u + 1) * 64],
                                    start=True, stop=True))

                @blk.vector
                def _(v):
                    fvB = Sem(s_fv2)
                    for c in range(NCH):
                        b = c % 2
                        for u in range(NU):
                            idx = c * 32 + u
                            r = idx % 2
                            g = u // 4
                            G = c * 8 + g
                            v.wait_ge(pe.h, pe0 + 3 * idx + 1)
                            if idx >= 2:
                                v.wait_ge(ac.h, ac0 + idx - 1)
                            dv.inc(v.tensor_scalar(
                                dl[:, r * 128:(r + 1) * 128],
                                plbc[:, (G % 2) * 512 + (u % 4) * 128:
                                     (G % 2) * 512 + (u % 4) * 128 + 128],
                                LTp[:, c * 32 + u:c * 32 + u + 1], 0.0,
                                ALU.subtract, ALU.min))
                            v.wait_ge(ac.h, ac0 + idx + 1)
                            fence(v, fvB, v.tensor_mul(
                                dbf[:, r * 128:(r + 1) * 128],
                                dbf[:, r * 128:(r + 1) * 128], tri01[:, :]))
                            fence(v, fvB, v.tensor_mul(
                                sbf[:, r * 128:(r + 1) * 128], pS[r][:, :],
                                dbf[:, r * 128:(r + 1) * 128]))
                            v.tensor_scalar(sbf[:, r * 128:(r + 1) * 128],
                                            sbf[:, r * 128:(r + 1) * 128],
                                            dtTp[:, c * 32 + u:c * 32 + u + 1],
                                            None, ALU.mult)
                            dv.inc(v.tensor_scalar(
                                b2[:, r * 64:(r + 1) * 64],
                                Ball[:, b * NU * 64 + u * 64:
                                     b * NU * 64 + (u + 1) * 64],
                                w2Tp[:, c * 32 + u:c * 32 + u + 1],
                                None, ALU.mult))
                            v.wait_ge(pe.h, pe0 + 3 * idx + 2)
                            if u == 0 and c >= 2:
                                v.wait_ge([s_y0, s_y1][c % 2], 16 * (c // 2))
                            fence(v, fvB, v.tensor_scalar(
                                ty[:, r * 64:(r + 1) * 64],
                                pY[r][:, 64:128],
                                eLTp[:, c * 32 + u:c * 32 + u + 1],
                                None, ALU.mult))
                            dv.inc(v.tensor_add(
                                ybuf[:, b * NU * 64 + u * 64:
                                     b * NU * 64 + (u + 1) * 64],
                                pY[r][:, 0:64], ty[:, r * 64:(r + 1) * 64]))
                            v.wait_ge(pe.h, pe0 + 3 * idx + 3)
                            fence(v, fvB, v.scalar_tensor_tensor(
                                stf[:, u * 64:(u + 1) * 64],
                                stf[:, u * 64:(u + 1) * 64],
                                arep[:, u * 32 + c:u * 32 + c + 1],
                                pSL[r][:, :], ALU.mult, ALU.add))
                            dv.inc(v.tensor_copy(stb[:, u * 64:(u + 1) * 64],
                                                 stf[:, u * 64:(u + 1) * 64]))

                @blk.scalar
                def _(sc):
                    for c in range(NCH):
                        for u in range(NU):
                            idx = c * 32 + u
                            r = idx % 2
                            sc.wait_ge(dv.h, dv0 + 4 * idx + 1)
                            ac.inc(sc.activation(dbf[:, r * 128:(r + 1) * 128],
                                                 dl[:, r * 128:(r + 1) * 128],
                                                 AF.Exp))

                @blk.gpsimd
                def _(g):
                    for c in range(NCH):
                        b = c % 2
                        g.wait_ge(dv.h, dv0 + 4 * (c * 32 + 31) + 3)
                        g.dma_start(
                            out=y[:, c * 128:(c + 1) * 128, :].rearrange(
                                "u s d -> s u d"),
                            in_=ybuf[:, b * NU * 64:(b + 1) * NU * 64
                                     ].rearrange("p (u d) -> p u d", d=64)
                            ).then_inc([s_y0, s_y1][c % 2], 16)
                    g.wait_ge(s_y0, 16 * 16)
                    g.wait_ge(s_y1, 16 * 16)
    return nc


def build_attn():
    nc = bass.Bass("TRN2", target_bir_lowering=False)
    NUA = 8
    NG = SEQ // 512
    qT = nc.dram_tensor("qT", [NUA, 128, SEQ], BF16, kind="ExternalInput")
    kT = nc.dram_tensor("kT", [NUA, 128, SEQ], BF16, kind="ExternalInput")
    vt = nc.dram_tensor("vt", [NUA, SEQ, 128], BF16, kind="ExternalInput")
    ctxT = nc.dram_tensor("ctxT", [NUA, 128, SEQ], BF16, kind="ExternalOutput")

    PEU = 313
    DVU = 67
    ACU = 147

    def pemmS(u, g, kc):
        return PEU * u + 18 + 4 * g * g + 5 * g + 2 * kc

    def plbcast(u, g):
        return PEU * u + 17 + 4 * g * g + 13 * g + 9

    def dv_grp(u, g):
        return DVU * u + 19 + 6 * (g + 1)

    es11 = ExitStack()
    with es11:
        s_ld = es11.enter_context(nc.semaphore("ld"))
        s_pe = es11.enter_context(nc.semaphore("pe"))
        s_dv = es11.enter_context(nc.semaphore("dv"))
        s_ac = es11.enter_context(nc.semaphore("ac"))
        s_st = es11.enter_context(nc.semaphore("st"))
        s_fv = es11.enter_context(nc.semaphore("fv"))
        s_u0 = es11.enter_context(nc.semaphore("u0"))
        s_u1 = es11.enter_context(nc.semaphore("u1"))
        s_g0 = es11.enter_context(nc.semaphore("g0"))
        s_g1 = es11.enter_context(nc.semaphore("g1"))
        qs = es11.enter_context(nc.sbuf_tensor("qs", [128, 2 * SEQ], BF16))
        ks = es11.enter_context(nc.sbuf_tensor("ks", [128, 2 * SEQ], BF16))
        vs = es11.enter_context(nc.sbuf_tensor("vs", [128, 2 * SEQ], BF16))
        sqb = es11.enter_context(nc.sbuf_tensor("sq", [128, SEQ], BF16))
        qn2 = es11.enter_context(nc.sbuf_tensor("qn2", [1, SEQ], F32))
        kn2 = es11.enter_context(nc.sbuf_tensor("kn2", [1, SEQ], F32))
        qm8 = es11.enter_context(nc.sbuf_tensor("qm8", [1, 8], F32))
        km1 = es11.enter_context(nc.sbuf_tensor("km1", [1, 1], F32))
        bm8 = es11.enter_context(nc.sbuf_tensor("bm8", [1, 8], F32))
        negb = es11.enter_context(nc.sbuf_tensor("negb", [128, 8], F32))
        pT = es11.enter_context(nc.sbuf_tensor("pT", [128, 2 * 512], BF16))
        lsb = es11.enter_context(nc.sbuf_tensor("lsb", [1, 512], F32))
        lrec = es11.enter_context(nc.sbuf_tensor("lrec", [128, 512], F32))
        ob = es11.enter_context(nc.sbuf_tensor("ob", [128, 2 * 512], BF16))
        tri = es11.enter_context(nc.sbuf_tensor("tri", [128, 4 * 512], BF16))
        iot5 = es11.enter_context(nc.sbuf_tensor("iot", [128, 512], F32))
        onesb = es11.enter_context(nc.sbuf_tensor("onesb", [128, 1], BF16))
        ones1 = es11.enter_context(nc.sbuf_tensor("ones1", [1, 128], F32))
        pS0 = es11.enter_context(nc.psum_tensor("pS0", [128, 512], F32))
        pS1 = es11.enter_context(nc.psum_tensor("pS1", [128, 512], F32))
        pctx = es11.enter_context(nc.psum_tensor("pctx", [128, 512], F32))
        pl = es11.enter_context(nc.psum_tensor("pl", [1, 512], F32))
        plr = es11.enter_context(nc.psum_tensor("plr", [128, 512], F32))
        pn = es11.enter_context(nc.psum_tensor("pn", [1, 512], F32))
        pS = [pS0, pS1]
        ld, pe, dv, ac, st = (Sem(x) for x in (s_ld, s_pe, s_dv, s_ac, s_st))
        with nc.Block() as blk:
            @blk.gpsimd
            def _(g):
                g.iota(iot5[:, :], [[1, 512]], channel_multiplier=-1,
                       allow_small_or_imprecise_dtypes=True)
                g.memset(onesb[:, :], 1.0)
                g.memset(ones1[:, :], 1.0)

        with nc.Block() as blk:
            @blk.vector
            def _(v):
                for j in range(4):
                    v.tensor_scalar(tri[:, j * 512:(j + 1) * 512], iot5[:, :],
                                    float(128 * j), 1.0, ALU.is_ge, ALU.mult)

        with nc.Block() as blk:
            @blk.sync
            def _(s):
                for u in range(NUA):
                    if u >= 2:
                        s.wait_ge(pe.h, PEU * (u - 1))
                        s.wait_ge(ac.h, ACU * (u - 1))
                    b = (u % 2) * SEQ
                    s.dma_start(out=qs[:, b:b + SEQ], in_=qT[u, :, :]
                                ).then_inc([s_u0, s_u1][u % 2], 16)
                    s.dma_start(out=ks[:, b:b + SEQ], in_=kT[u, :, :]
                                ).then_inc([s_u0, s_u1][u % 2], 16)
                    s.dma_start(out=vs[:, b:b + SEQ].rearrange(
                                    "p (kc d) -> p kc d", d=128),
                                in_=vt[u, :, :].rearrange(
                                    "(kc p) d -> p kc d", p=128)
                                ).then_inc([s_u0, s_u1][u % 2], 16)

            @blk.tensor
            def _(te):
                for u in range(NUA):
                    b = (u % 2) * SEQ
                    te.wait_ge([s_u0, s_u1][u % 2], 48 * (u // 2 + 1))
                    te.wait_ge(ac.h, ACU * u + 1)
                    for j in range(8):
                        if u + j > 0:
                            te.wait_ge(dv.h, DVU * u + j)
                        te.matmul(pn[:, :], onesb[:, :],
                                  sqb[:, j * 512:(j + 1) * 512],
                                  start=True, stop=True).then_inc(pe.h, 1)
                    te.wait_ge(ac.h, ACU * u + 2)
                    for j in range(8):
                        te.wait_ge(dv.h, DVU * u + 8 + j)
                        te.matmul(pn[:, :], onesb[:, :],
                                  sqb[:, j * 512:(j + 1) * 512],
                                  start=True, stop=True).then_inc(pe.h, 1)
                    te.wait_ge(dv.h, DVU * u + 18)
                    te.matmul(plr[:, 0:8], ones1[0:1, :], bm8[0:1, :],
                              start=True, stop=True).then_inc(pe.h, 1)
                    pe.n = PEU * u + 17
                    kci = 0
                    for g in range(NG):
                        nkc = 4 * g + 4
                        for kc in range(nkc):
                            r = kci % 2
                            if kci >= 2:
                                te.wait_ge(ac.h, ACU * u + kci + 2)
                            te.matmul(pS[r][:, :],
                                      ks[:, b + kc * 128:b + (kc + 1) * 128],
                                      qs[:, b + g * 512:b + (g + 1) * 512],
                                      start=True, stop=True).then_inc(pe.h, 1)
                            if kc == 0:
                                te.wait_ge(dv.h, DVU * u + 19 + 6 * g)
                            if kc >= 4 * g:
                                j = kc - 4 * g
                                te.wait_ge(dv.h,
                                           DVU * u + 19 + 6 * g + j + 1)
                            else:
                                te.wait_ge(ac.h, ACU * u + kci + 4)
                            te.matmul(pctx[:, :],
                                      vs[:, b + kc * 128:b + (kc + 1) * 128],
                                      pT[:, r * 512:(r + 1) * 512],
                                      start=(kc == 0), stop=(kc == nkc - 1),
                                      skip_group_check=True)
                            te.matmul(pl[:, :], onesb[:, :],
                                      pT[:, r * 512:(r + 1) * 512],
                                      start=(kc == 0), stop=(kc == nkc - 1),
                                      skip_group_check=True).then_inc(pe.h, 1)
                            kci += 1
                        te.wait_ge(dv.h, DVU * u + 19 + 6 * g + 5)
                        te.matmul(plr[:, :], ones1[0:1, :], lsb[0:1, :],
                                  start=True, stop=True).then_inc(pe.h, 1)
                    pe.n = PEU * (u + 1)

            @blk.scalar
            def _(sc):
                for u in range(NUA):
                    b = (u % 2) * SEQ
                    sc.wait_ge([s_u0, s_u1][u % 2], 48 * (u // 2 + 1))
                    if u >= 1:
                        sc.wait_ge(pe.h, PEU * (u - 1) + 16)
                    sc.activation(sqb[:, :], qs[:, b:b + SEQ], AF.Square
                                  ).then_inc(ac.h, 1)
                    sc.wait_ge(pe.h, PEU * u + 8)
                    sc.activation(sqb[:, :], ks[:, b:b + SEQ], AF.Square
                                  ).then_inc(ac.h, 1)
                    sc.wait_ge(dv.h, DVU * u + 17)
                    sc.activation(bm8[:, :], bm8[:, :], AF.Sqrt
                                  ).then_inc(ac.h, 1)
                    kci = 0
                    for g in range(NG):
                        nkc = 4 * g + 4
                        for kc in range(nkc):
                            r = kci % 2
                            sc.wait_ge(pe.h, pemmS(u, g, kc))
                            if kci == 0:
                                sc.wait_ge(dv.h, DVU * u + 19)
                            sc.activation(pT[:, r * 512:(r + 1) * 512],
                                          pS[r][:, :], AF.Exp,
                                          bias=negb[:, g:g + 1]
                                          ).then_inc(ac.h, 1)
                            kci += 1
                    ac.n = ACU * (u + 1)

            @blk.vector
            def _(v):
                fv = Sem(s_fv)
                for u in range(NUA):
                    b = (u % 2) * SEQ
                    for j in range(8):
                        v.wait_ge(pe.h, PEU * u + j + 1)
                        v.tensor_copy(qn2[0:1, j * 512:(j + 1) * 512],
                                      pn[0:1, :]).then_inc(dv.h, 1)
                    for j in range(8):
                        v.wait_ge(pe.h, PEU * u + 8 + j + 1)
                        ins4 = v.tensor_copy(kn2[0:1, j * 512:(j + 1) * 512],
                                             pn[0:1, :])
                        ins4.then_inc(dv.h, 1)
                    fence(v, fv, ins4) if False else None
                    v.wait_ge(dv.h, DVU * u + 16)
                    for g in range(NG):
                        v.reduce_max(qm8[0:1, g:g + 1],
                                     qn2[0:1, g * 512:(g + 1) * 512],
                                     axis=AX.X)
                    fence(v, fv, v.reduce_max(km1[0:1, :], kn2[0:1, :],
                                              axis=AX.X))
                    v.tensor_tensor(bm8[0:1, :], qm8[0:1, :],
                                    bass.AP(km1, 0, [[1, 1], [0, 8]]),
                                    ALU.mult).then_inc(dv.h, 1)
                    v.wait_ge(ac.h, ACU * u + 3)
                    v.tensor_scalar_mul(bm8[0:1, :], bm8[0:1, :], -1.0
                                        ).then_inc(dv.h, 1)
                    v.wait_ge(pe.h, PEU * u + 17)
                    v.tensor_copy(negb[:, :], plr[:, 0:8]).then_inc(dv.h, 1)
                    dv.n = DVU * u + 19
                    kci = 0
                    for g in range(NG):
                        nkc = 4 * g + 4
                        for j in range(4):
                            kci_m = 4 * g + j
                            kabs = kci + 4 * g + j
                            v.wait_ge(ac.h, ACU * u + 4 + kabs)
                            r = kabs % 2
                            v.tensor_mul(pT[:, r * 512:(r + 1) * 512],
                                         pT[:, r * 512:(r + 1) * 512],
                                         tri[:, j * 512:(j + 1) * 512]
                                         ).then_inc(dv.h, 1)
                        v.wait_ge(pe.h, pemmS(u, g, nkc - 1) + 1)
                        v.tensor_copy(lsb[0:1, :], pl[0:1, :]).then_inc(dv.h, 1)
                        v.wait_ge(pe.h, plbcast(u, g))
                        if u * 8 + g >= 2:
                            v.wait_ge([s_g0, s_g1][g % 2],
                                      16 * ((u * 8 + g) // 2))
                        fence(v, fv, v.reciprocal(lrec[:, :], plr[:, :]))
                        v.tensor_mul(ob[:, (g % 2) * 512:(g % 2) * 512 + 512],
                                     pctx[:, :], lrec[:, :]).then_inc(dv.h, 1)
                        kci += nkc
                    dv.n = DVU * (u + 1)

            @blk.gpsimd
            def _(g_):
                for u in range(NUA):
                    for g in range(NG):
                        g_.wait_ge(dv.h, dv_grp(u, g))
                        g_.dma_start(
                            out=ctxT[u, :, g * 512:(g + 1) * 512],
                            in_=ob[:, (g % 2) * 512:(g % 2) * 512 + 512]
                            ).then_inc([s_g0, s_g1][g % 2], 16)
                g_.wait_ge(s_g0, 16 * 32)
                g_.wait_ge(s_g1, 16 * 32)
    return nc


def _run(nc, in_maps):
    res = run_bass_kernel_spmd(nc, in_maps, core_ids=list(range(NCORES)))
    return res.results


def _rep(vec):
    v = np.ascontiguousarray(np.asarray(vec), dtype=np.float32).reshape(1, -1)
    return np.ascontiguousarray(np.broadcast_to(v, (128, v.shape[1])))


def _f32c(a):
    return np.ascontiguousarray(a, dtype=np.float32)


def _cat(results, key):
    return np.concatenate([r[key] for r in results], axis=0)


def kernel(L0_attn_hidden_states, L0_attn_ln_w, L0_attn_ln_b,
           L0_attn_in_proj_w, L0_attn_dt_bias, L0_attn_A_log, L0_attn_Dp,
           L0_attn_norm_w, L0_attn_out_proj_w, L0_attn_fc1_w, L0_attn_fc1_b,
           L0_attn_fc2_w, L0_attn_fc2_b, L0_ffn_router_w, L0_ffn_router_b,
           L0_ffn_expert_w, L0_ffn_expert_b, L1_attn_w_qkv, L1_attn_b_qkv,
           L1_attn_w_o, L1_ffn_lut, L1_ffn_gate_idx, L1_ffn_up_idx,
           L1_ffn_down_idx):
    NT = NCORES * TT
    X = _f32c(np.asarray(L0_attn_hidden_states)).reshape(NT, D)

    def shards(arr):
        return [np.ascontiguousarray(arr[c * TT:(c + 1) * TT])
                for c in range(NCORES)]

    def bcastmaps(per_core, shared):
        return [dict(per_core[c], **shared) for c in range(NCORES)]

    def tr(a):
        return np.ascontiguousarray(np.asarray(a).T)

    nc_ln0 = build_add_ln(two=False, gamma=True, out_bf=True)
    r = _run(nc_ln0, bcastmaps(
        [{"a": s} for s in shards(X)],
        {"lnw": _rep(L0_attn_ln_w), "lnb": _rep(L0_attn_ln_b)}))
    h_bf = [rr["ob"] for rr in r]

    nc_ip = build_proj("inproj", DINPROJ, True, extra="inproj")
    ipT = _f32c(np.asarray(L0_attn_in_proj_w).T)
    r = _run(nc_ip, bcastmaps(
        [{"xT": tr(h_bf[c])} for c in range(NCORES)],
        {"wT": ipT, "dtb": _rep(np.asarray(L0_attn_dt_bias))}))
    zxbc = _cat(r, "mm")
    dtg = _cat(r, "dt")

    zg = zxbc[:, 0:4096]
    xg = zxbc[:, 4096:5120].reshape(BATCH, SEQ, NKV, HDIM)
    bg = zxbc[:, 5120:6144].reshape(BATCH, SEQ, NKV, HDIM)
    cg = zxbc[:, 6144:10240].reshape(BATCH, SEQ, NHEADS, HDIM)
    dtg4 = dtg.reshape(BATCH, SEQ, NHEADS)
    alog = np.asarray(L0_attn_A_log, dtype=np.float32)
    per_core = []
    for c in range(NCORES):
        us = [divmod(c * 32 + i, NHEADS) for i in range(32)]
        xt = np.stack([xg[b, :, h // 4, :] for b, h in us])
        bt = np.stack([bg[b, :, h // 4, :] for b, h in us])
        ct = np.stack([cg[b, :, h, :] for b, h in us])
        per_core.append({
            "dtin": _f32c(np.stack([dtg4[b, :, h] for b, h in us])),
            "alog": _f32c(np.array([[alog[h]] for b, h in us])),
            "xt": np.ascontiguousarray(xt),
            "bt": np.ascontiguousarray(bt),
            "bfm": np.ascontiguousarray(bt.transpose(0, 2, 1)),
            "cfm": np.ascontiguousarray(ct.transpose(0, 2, 1)),
        })
    nc_scan = build_scan()
    r = _run(nc_scan, per_core)
    yu = np.concatenate([rr["y"] for rr in r], axis=0)
    y_tok = np.ascontiguousarray(
        yu.reshape(BATCH, NHEADS, SEQ, HDIM).transpose(0, 2, 1, 3)
        .reshape(NT, D_INNER))
    xh_tok = np.ascontiguousarray(
        np.repeat(xg, NHEADS // NKV, axis=2).reshape(NT, D_INNER))

    nc_g0 = build_gate0()
    dp64 = np.repeat(np.asarray(L0_attn_Dp, dtype=np.float32), HDIM)
    r = _run(nc_g0, bcastmaps(
        [{"y": np.ascontiguousarray(y_tok[c * TT:(c + 1) * TT]),
          "xh": np.ascontiguousarray(xh_tok[c * TT:(c + 1) * TT]),
          "z": np.ascontiguousarray(zg[c * TT:(c + 1) * TT])}
         for c in range(NCORES)],
        {"dpr": _rep(dp64), "nwr": _rep(np.asarray(L0_attn_norm_w))}))
    yf = [rr["yf"] for rr in r]

    nc_mid = build_fm_proj("mid0", INTER0, True, dual=False, act="gelu_bias")
    fc1T = _f32c(np.asarray(L0_attn_fc1_w).T)
    fc1b = _f32c(np.asarray(L0_attn_fc1_b).reshape(INTER0 // 128, 128).T)
    r = _run(nc_mid, bcastmaps(
        [{"xT": tr(h_bf[c])} for c in range(NCORES)],
        {"w1": fc1T, "w2": np.zeros((1, 1), np.float32)[:0],
         "bvec": fc1b} if False else
        {"w1": fc1T, "bvec": fc1b}))
    midT = [rr["midT"] for rr in r]

    nc_l0mm = build_tail_mm("l0mm", [D_INNER, INTER0], True, with_bias=True)
    opT = _f32c(np.asarray(L0_attn_out_proj_w).T)
    fc2T = _f32c(np.asarray(L0_attn_fc2_w).T)
    r = _run(nc_l0mm, bcastmaps(
        [{"xT0": tr(yf[c]), "xT1": midT[c]} for c in range(NCORES)],
        {"wT0": opT, "wT1": fc2T, "bias": _rep(np.asarray(L0_attn_fc2_b))}))
    l0mm = [rr["mm"] for rr in r]

    nc_ln1 = build_add_ln(two=True, gamma=False, out_bf=True, double_b=True)
    nc_ln = build_add_ln(two=True, gamma=False, out_bf=True)
    r = _run(nc_ln1, [{"a": l0mm[c], "b": s}
                      for c, s in enumerate(shards(X))])
    x1 = [rr["o"] for rr in r]
    x1b = [rr["ob"] for rr in r]

    nc_rt = build_router_v2()
    rwT = _f32c(np.asarray(L0_ffn_router_w).T)
    r = _run(nc_rt, bcastmaps(
        [{"xT": tr(x1b[c])} for c in range(NCORES)],
        {"wT": rwT, "brep": _rep(np.asarray(L0_ffn_router_b))}))
    cg_r = [rr["c"] for rr in r]

    nc_moe = build_moe_mm()
    ewT = _f32c(np.asarray(L0_ffn_expert_w).transpose(0, 2, 1))
    r = _run(nc_moe, bcastmaps(
        [{"xT": tr(x1b[c]), "c": cg_r[c], "cT": tr(cg_r[c])}
         for c in range(NCORES)],
        {"ewT": ewT, "eb": _f32c(np.asarray(L0_ffn_expert_b))}))
    moemm = [rr["mm"] for rr in r]

    r = _run(nc_ln, [{"a": x1[c], "b": moemm[c]} for c in range(NCORES)])
    x2 = [rr["o"] for rr in r]
    x2b = [rr["ob"] for rr in r]

    nc_qkv = build_proj("qkv", 3 * D, True, extra="qkv")
    r = _run(nc_qkv, bcastmaps(
        [{"xT": tr(x2b[c])} for c in range(NCORES)],
        {"wT": _f32c(np.asarray(L1_attn_w_qkv)),
         "bias": _rep(np.asarray(L1_attn_b_qkv))}))
    qkv = _cat(r, "mm").reshape(BATCH, SEQ, 3 * D)

    per_core = []
    for c in range(NCORES):
        us = [divmod(c * 8 + i, AHEADS) for i in range(8)]
        per_core.append({
            "qT": np.ascontiguousarray(np.stack(
                [qkv[b, :, hd * AHD:(hd + 1) * AHD].T for b, hd in us])),
            "kT": np.ascontiguousarray(np.stack(
                [qkv[b, :, D + hd * AHD:D + (hd + 1) * AHD].T
                 for b, hd in us])),
            "vt": np.ascontiguousarray(np.stack(
                [qkv[b, :, 2 * D + hd * AHD:2 * D + (hd + 1) * AHD]
                 for b, hd in us])),
        })
    nc_at = build_attn()
    r = _run(nc_at, per_core)
    ctxu = np.concatenate([rr["ctxT"] for rr in r], axis=0)
    ctxg = ctxu.reshape(BATCH, AHEADS, AHD, SEQ)

    nc_ao = build_proj("attnout", D, True, extra=None)
    per_core = []
    for c in range(NCORES):
        b, half = divmod(c, 2)
        sl = slice(half * TT, (half + 1) * TT)
        ctxT_c = np.ascontiguousarray(
            ctxg[b, :, :, sl].reshape(D, TT))
        per_core.append({"xT": ctxT_c})
    r = _run(nc_ao, bcastmaps(per_core,
                              {"wT": _f32c(np.asarray(L1_attn_w_o))}))
    aomm = [rr["mm"] for rr in r]

    r = _run(nc_ln, [{"a": x2[c], "b": aomm[c]} for c in range(NCORES)])
    x3 = [rr["o"] for rr in r]
    x3b = [rr["ob"] for rr in r]

    gT = np.ascontiguousarray(np.asarray(L1_ffn_gate_idx).T)
    uT = np.ascontiguousarray(np.asarray(L1_ffn_up_idx).T)
    dT = np.ascontiguousarray(np.asarray(L1_ffn_down_idx).T)
    dTp = np.zeros((6144, D), np.int32)
    dTp[:GINTER] = dT
    lut2 = _f32c(np.asarray(L1_ffn_lut)).reshape(1, 16)
    per_core = []
    for c in range(NCORES):
        per_core.append({
            "gsl": np.ascontiguousarray(gT[c * 256:(c + 1) * 256]),
            "usl": np.ascontiguousarray(uT[c * 256:(c + 1) * 256]),
            "dsl": np.ascontiguousarray(dTp[c * 768:(c + 1) * 768]),
            "lut": lut2,
        })
    nc_wb = build_wbuild()
    r = _run(nc_wb, per_core)
    WgT = _cat(r, "go")
    WuT = _cat(r, "uo")
    WdT = _cat(r, "do")[:GINTER]

    nc_gm = build_fm_proj("gmid", GINTER, False, dual=True, act=None)
    r = _run(nc_gm, bcastmaps(
        [{"xT": tr(x3b[c])} for c in range(NCORES)],
        {"w1": WgT, "w2": WuT}))
    gmidT = [rr["midT"] for rr in r]

    nc_gd = build_tail_mm("gdown", [GINTER], False, with_bias=False)
    r = _run(nc_gd, bcastmaps([{"xT0": gmidT[c]} for c in range(NCORES)],
                              {"wT0": np.ascontiguousarray(WdT)}))
    gdmm = [rr["mm"] for rr in r]

    nc_lnf = build_add_ln(two=True, gamma=False, out_bf=False)
    r = _run(nc_lnf, [{"a": x3[c], "b": gdmm[c]} for c in range(NCORES)])
    out = np.concatenate([rr["o"] for rr in r], axis=0)
    return out.reshape(BATCH, SEQ, D).astype(np.float32)
